# version 03
# baseline (speedup 1.0000x reference)
import math
from contextlib import ExitStack

import numpy as np
import ml_dtypes

import concourse.bass as bass
import concourse.mybir as mybir
from concourse.bass_utils import run_bass_kernel_spmd

F32 = mybir.dt.float32
BF16 = mybir.dt.bfloat16
I8 = mybir.dt.int8
AF = mybir.ActivationFunctionType
ALU = mybir.AluOpType
AX = mybir.AxisListType
P = 128
EPS = 1e-5
LN2 = math.log(2.0)


def full_cfg():
    return dict(D=2048, TOWN=2048, TKV=4096, CH=512, F=8192, NCORES=8)


def small_cfg():
    return dict(D=256, TOWN=256, TKV=512, CH=128, F=512, NCORES=2)


ENGINES = ("tensor", "scalar", "vector", "gpsimd", "sync")


KPOOL = 6
DQUEUES = ("sync", "gpsimd", "scalar")


class Ctx:
    def __init__(self, nc, es):
        self.nc = nc
        self.esem = {e: es.enter_context(nc.semaphore("es_" + e)) for e in ENGINES}
        self.dsem = {q: [es.enter_context(nc.semaphore(f"ds_{q}{i}")) for i in range(KPOOL)]
                     for q in DQUEUES}
        self.ecount = {e: 0 for e in ENGINES}
        self.dcount = {q: 0 for q in DQUEUES}
        self.streams = {e: [] for e in ENGINES}
        self.observed = {e: {} for e in ENGINES}
        self.regions = {}

    def _slot_total(self, q, slot):
        n = self.dcount[q]
        if n <= slot:
            return 0
        return 16 * ((n - 1 - slot) // KPOOL + 1)

    def _edges(self, eng, reads, writes):
        need = {}

        def add(tok):
            if tok is None:
                return
            kind, src, cnt = tok
            if kind == "e" and src == "tensor" and eng == "tensor":
                return
            if kind == "d":
                q, n = src, cnt
                slot = n % KPOOL
                k = ("d", q, slot)
                v = 16 * (n // KPOOL + 1)
                if need.get(k, 0) < v:
                    need[k] = v
                return
            k = (kind, src)
            if need.get(k, 0) < cnt:
                need[k] = cnt

        for key in reads:
            r = self.regions.get(key)
            if r is not None:
                add(r["w"])
        for key in writes:
            r = self.regions.get(key)
            if r is not None:
                add(r["w"])
                for t in r["r"]:
                    add(t)
        waits = []
        obs = self.observed[eng]
        for k, cnt in sorted(need.items(), key=str):
            if obs.get(k, 0) >= cnt:
                continue
            if k[0] == "e":
                sem = self.esem[k[1]]
            else:
                sem = self.dsem[k[1]][k[2]]
            waits.append((sem, cnt))
            obs[k] = cnt
        return waits

    def _record(self, eng, waits, emit):
        def run(engobj, waits=waits, emit=emit):
            for sem, cnt in waits:
                engobj.wait_ge(sem, cnt)
            emit(engobj)

        self.streams[eng].append(run)

    def op(self, eng, fn, reads=(), writes=()):
        waits = self._edges(eng, reads, writes)
        self.ecount[eng] += 1
        tok = ("e", eng, self.ecount[eng])
        sem = self.esem[eng]

        def emit(engobj, fn=fn, sem=sem):
            fn(engobj).then_inc(sem, 1)

        self._record(eng, waits, emit)
        for key in writes:
            self.regions[key] = {"w": tok, "r": []}
        for key in reads:
            self.regions.setdefault(key, {"w": None, "r": []})["r"].append(tok)
        return tok

    def dma(self, q, out, in_, reads=(), writes=()):
        waits = self._edges(q, reads, writes)
        n = self.dcount[q]
        slot = n % KPOOL
        if n >= KPOOL:
            k = ("d", q, slot)
            v = 16 * ((n - KPOOL) // KPOOL + 1)
            obs = self.observed[q]
            if obs.get(k, 0) < v:
                waits = waits + [(self.dsem[q][slot], v)]
                obs[k] = v
        self.dcount[q] = n + 1
        tok = ("d", q, n)
        sem = self.dsem[q][slot]

        def emit(engobj, out=out, in_=in_, sem=sem):
            engobj.dma_start(out=out, in_=in_).then_inc(sem, 16)

        self._record(q, waits, emit)
        for key in writes:
            self.regions[key] = {"w": tok, "r": []}
        for key in reads:
            self.regions.setdefault(key, {"w": None, "r": []})["r"].append(tok)
        return tok

    def raw(self, eng, fn):
        self.streams[eng].append(fn)

    def barrier(self):
        toks = [(("e", e), self.ecount[e], self.esem[e]) for e in ENGINES if self.ecount[e] > 0]
        for q in DQUEUES:
            for slot in range(KPOOL):
                v = self._slot_total(q, slot)
                if v > 0:
                    toks.append((("d", q, slot), v, self.dsem[q][slot]))
        for eng in ENGINES:
            obs = self.observed[eng]
            waits = []
            for k, cnt, sem in toks:
                if k == ("e", eng):
                    continue
                if obs.get(k, 0) >= cnt:
                    continue
                waits.append((sem, cnt))
                obs[k] = cnt
            if waits:
                self._record(eng, waits, lambda e: None)

    def finalize(self, block):
        for eng in ENGINES:
            ops = self.streams[eng]
            getattr(block, eng)(lambda engobj, ops=ops: [f(engobj) for f in ops])


def build_program(cfg, dumps=()):
    D = cfg["D"]; TOWN = cfg["TOWN"]; TKV = cfg["TKV"]; CH = cfg["CH"]; F = cfg["F"]
    KD = D // P
    NTT = TOWN // P
    NTB = TKV // P
    NCO = TOWN // CH
    NCK = TKV // CH
    TGAP = TKV - TOWN
    SLOTS = CH // P
    HA = D // 64
    HB = D // P
    NF = F // P
    DCH = D // CH
    HALF = KD * TOWN

    def jbmax(ic):
        return (TGAP + (ic + 1) * CH) // P - 1

    def bandst(ic):
        return (TGAP + ic * CH) // P

    nc = bass.Bass("TRN2", target_bir_lowering=False, debug=False)

    def din(name, shape, dt=F32):
        return nc.dram_tensor(name, list(shape), dt, kind="ExternalInput").ap()

    xown = din("xown", (TOWN, D))
    xkv = din("xkv", (TKV, D))
    wsrc = {nm: din(nm, (D, D)) for nm in ("wqa", "wka", "wva", "woa", "wqb", "wvb", "wob", "wgb")}
    wf = din("wf", (HA, D)); bfv = din("bfv", (HA, 1))
    lng = din("lng", (1, D)); lnb = din("lnb", (1, D)); betab = din("betab", (1, HB))
    wg8 = din("wg8", (D, F), I8); wu8 = din("wu8", (D, F), I8); wd8 = din("wd8", (F, D), I8)
    sg = din("sg", (1, F)); su = din("su", (1, F)); sd = din("sd", (1, D))
    identb = din("identb", (P, P), BF16)
    identf = din("identf", (P, P), F32)
    mtri = din("mtri", (SLOTS * P, CH), BF16)
    rolev = din("rolev", (P, 1))
    mpref = din("mpref", (P, (TKV // P) * (D // 64)))
    widx = din("widx", (P, 1), mybir.dt.int32)
    ridx = din("ridx", (P, 1), mybir.dt.int32)
    noncev = din("noncev", (P, 16), mybir.dt.int32)

    out = nc.dram_tensor("out", [TOWN, D], F32, kind="ExternalOutput").ap()
    dump_aps = {nm: nc.dram_tensor(nm, [TOWN, D], F32, kind="ExternalOutput").ap() for nm in dumps}

    scr_x = nc.dram_tensor("scr_x", [TOWN, D], F32).ap()
    scr_q = nc.dram_tensor("scr_q", [HA * 67, TOWN], BF16).ap()
    scr_k = nc.dram_tensor("scr_k", [HA * 64, TKV], BF16).ap()
    scr_v = nc.dram_tensor("scr_v", [TKV, HA * 64], BF16).ap()
    scr_g = nc.dram_tensor("scr_g", [TOWN, D], BF16).ap()
    scr_qb = nc.dram_tensor("scr_qb", [D, TOWN], BF16).ap()
    scr_vb = nc.dram_tensor("scr_vb", [D, TOWN], BF16).ap()
    wT = {nm: nc.dram_tensor("scrT_" + nm, [KD * KD * P, P], BF16).ap() for nm in wsrc}
    scr_wg = nc.dram_tensor("scr_wg", [NF * KD * P, P], BF16).ap()
    scr_wu = nc.dram_tensor("scr_wu", [NF * KD * P, P], BF16).ap()
    scr_wd = nc.dram_tensor("scr_wd", [F, D], BF16).ap()
    NPAIR = max(1, cfg["NCORES"] // 2)
    shc = nc.dram_tensor("shc", [(NPAIR + 1) * P, HB], F32, addr_space="Shared").ap()
    shf = nc.dram_tensor("shf", [(NPAIR + 1) * P, 16], mybir.dt.int32, addr_space="Shared").ap()

    es = ExitStack()
    ctx = Ctx(nc, es)
    ccsem = es.enter_context(nc.semaphore("ccsem"))

    identb_sb = es.enter_context(nc.sbuf_tensor("identb_sb", [P, P], BF16))
    identf_sb = es.enter_context(nc.sbuf_tensor("identf_sb", [P, P], F32))
    onesf = es.enter_context(nc.sbuf_tensor("onesf", [1, P], F32))
    mtri_sb = es.enter_context(nc.sbuf_tensor("mtri_sb", [P, SLOTS * CH], BF16))
    rolev_sb = es.enter_context(nc.sbuf_tensor("rolev_sb", [P, 1], F32))
    rolenv_sb = es.enter_context(nc.sbuf_tensor("rolenv_sb", [P, 1], F32))

    ctx.dma("sync", identb_sb[:], identb[:], writes=[("identb",)])
    ctx.dma("sync", identf_sb[:], identf[:], writes=[("identf",)])
    ctx.dma("sync", rolev_sb[:], rolev[:], writes=[("rolev",)])
    for s in range(SLOTS):
        ctx.dma("sync", mtri_sb[:, s * CH:(s + 1) * CH], mtri[s * P:(s + 1) * P, :], writes=[("mtri", s)])
    ctx.op("vector", lambda v: v.memset(onesf[:], 1.0), writes=[("onesf",)])
    ctx.op("vector", lambda v: v.tensor_scalar(out=rolenv_sb[:], in0=rolev_sb[:], scalar1=-1.0, scalar2=1.0,
                                               op0=ALU.mult, op1=ALU.add),
           reads=[("rolev",)], writes=[("rolenv",)])

    ar = {}

    def Akv(k, c0, c1):
        return ar["a"][:, k * TKV + c0: k * TKV + c1]

    def Ao(k, c0, c1):
        return ar["a"][:, k * TOWN + c0: k * TOWN + c1]

    def Bv(c0, c1, r0=0, r1=P):
        return ar["a"][r0:r1, HALF + c0: HALF + c1]

    def Wr1(k, c0, c1):
        return ar["a"][:, k * D + c0: k * D + c1]

    def Wr2(k, c0, c1):
        return ar["a"][:, HALF + k * D + c0: HALF + k * D + c1]

    def prep():
        ps = ExitStack()
        s8 = [ps.enter_context(nc.sbuf_tensor(f"p_s8_{i}", [P, F], I8)) for i in range(2)]
        b16 = [ps.enter_context(nc.sbuf_tensor(f"p_b16_{i}", [P, F], BF16)) for i in range(2)]
        scbc = ps.enter_context(nc.sbuf_tensor("p_scbc", [P, F], F32))
        scrow = ps.enter_context(nc.sbuf_tensor("p_scrow", [1, F], F32))
        psb = ps.enter_context(nc.psum_tensor("p_psb", [P, CH], F32))
        wrow = [ps.enter_context(nc.sbuf_tensor(f"p_wrow_{i}", [P, D], F32)) for i in range(2)]
        wrowb = [ps.enter_context(nc.sbuf_tensor(f"p_wrowb_{i}", [P, D], BF16)) for i in range(2)]
        pstr = [ps.enter_context(nc.psum_tensor(f"p_pstr_{i}", [P, P], BF16)) for i in range(4)]
        tbuf = [ps.enter_context(nc.sbuf_tensor(f"p_tbuf_{i}", [P, P], BF16)) for i in range(4)]

        for nm, src, scale, dst, rows, cols, tiled in (
            ("wg", wg8, sg, scr_wg, KD, F, True),
            ("wu", wu8, su, scr_wu, KD, F, True),
            ("wd", wd8, sd, scr_wd, NF, D, False),
        ):
            ctx.dma("sync", scrow[:, 0:cols], scale[:, 0:cols], writes=[("scrow",)])
            for c in range(0, cols, CH):
                ctx.op("tensor", lambda t, c=c: t.matmul(out=psb[:], lhsT=onesf[:], rhs=scrow[0:1, c:c + CH],
                                                         start=True, stop=True),
                       reads=[("onesf",), ("scrow",)], writes=[("p_psb",)])
                ctx.op("scalar", lambda s, c=c: s.copy(out=scbc[:, c:c + CH], in_=psb[:]),
                       reads=[("p_psb",)], writes=[("scbc", c // CH)])
            sckeys = [("scbc", c) for c in range(cols // CH)]
            for kt in range(rows):
                sl = kt % 2
                ctx.dma("sync", s8[sl][:, 0:cols], src[kt * P:(kt + 1) * P, 0:cols], writes=[("p_s8", sl)])
                ctx.op("vector", lambda v, sl=sl, cols=cols: v.tensor_copy(out=b16[sl][:, 0:cols],
                                                                           in_=s8[sl][:, 0:cols]),
                       reads=[("p_s8", sl)], writes=[("p_b16", sl)])
                ctx.op("vector", lambda v, sl=sl, cols=cols: v.tensor_tensor(out=b16[sl][:, 0:cols],
                                                                             in0=b16[sl][:, 0:cols],
                                                                             in1=scbc[:, 0:cols], op=ALU.mult),
                       reads=[("p_b16", sl)] + sckeys, writes=[("p_b16", sl)])
                if tiled:
                    for ft in range(cols // P):
                        ctx.dma("gpsimd", dst[(ft * rows + kt) * P:(ft * rows + kt + 1) * P, :],
                                b16[sl][:, ft * P:(ft + 1) * P],
                                reads=[("p_b16", sl)], writes=[("scrw_" + nm, kt, ft)])
                else:
                    ctx.dma("gpsimd", dst[kt * P:(kt + 1) * P, :], b16[sl][:, 0:cols],
                            reads=[("p_b16", sl)], writes=[("scrw_" + nm, kt)])

        for nm in wsrc:
            src = wsrc[nm]; dst = wT[nm]
            for dot in range(KD):
                sl = dot % 2
                ctx.dma("sync", wrow[sl][:], src[dot * P:(dot + 1) * P, :], writes=[("p_wrow", sl)])
                ctx.op("scalar", lambda s, sl=sl: s.copy(out=wrowb[sl][:], in_=wrow[sl][:]),
                       reads=[("p_wrow", sl)], writes=[("p_wrowb", sl)])
                for dib in range(KD):
                    j = dib % 4
                    ctx.op("tensor", lambda t, sl=sl, j=j, dib=dib:
                           t.transpose(out=pstr[j][:], in_=wrowb[sl][:, dib * P:(dib + 1) * P],
                                       identity=identb_sb[:]),
                           reads=[("p_wrowb", sl), ("identb",)], writes=[("p_pstr", j)])
                    if dib % 2 == 0:
                        ctx.op("scalar", lambda s, j=j: s.copy(out=tbuf[j][:], in_=pstr[j][:]),
                               reads=[("p_pstr", j)], writes=[("p_tbuf", j)])
                    else:
                        ctx.op("vector", lambda v, j=j: v.tensor_copy(out=tbuf[j][:], in_=pstr[j][:]),
                               reads=[("p_pstr", j)], writes=[("p_tbuf", j)])
                    ctx.dma("gpsimd", dst[(dot * KD + dib) * P:(dot * KD + dib + 1) * P, :], tbuf[j][:],
                            reads=[("p_tbuf", j)], writes=[("wT" + nm, dot, dib)])
        ctx.barrier()
        ps.close()

    def ln_rows(xin, st, xb_out, rk, wk, tag, dmp, dmp_key):
        ctx.op("vector", lambda v: v.reduce_sum(out=st[:, 0:1], in_=xin, axis=AX.X),
               reads=rk, writes=[(tag, 0)])
        ctx.op("scalar", lambda s: s.activation(out=dmp, in_=xin, func=AF.Square, accum_out=st[:, 1:2]),
               reads=rk, writes=[dmp_key, (tag, 1)])
        ctx.op("vector", lambda v: v.tensor_scalar(out=st[:, 2:3], in0=st[:, 0:1], scalar1=1.0 / D,
                                                   scalar2=None, op0=ALU.mult),
               reads=[(tag, 0)], writes=[(tag, 2)])
        ctx.op("vector", lambda v: v.tensor_tensor(out=st[:, 3:4], in0=st[:, 2:3], in1=st[:, 2:3], op=ALU.mult),
               reads=[(tag, 2)], writes=[(tag, 3)])
        ctx.op("vector", lambda v: v.tensor_scalar(out=st[:, 4:5], in0=st[:, 1:2], scalar1=1.0 / D,
                                                   scalar2=None, op0=ALU.mult),
               reads=[(tag, 1)], writes=[(tag, 4)])
        ctx.op("vector", lambda v: v.tensor_tensor(out=st[:, 5:6], in0=st[:, 4:5], in1=st[:, 3:4], op=ALU.subtract),
               reads=[(tag, 3), (tag, 4)], writes=[(tag, 5)])
        ctx.op("vector", lambda v: v.tensor_scalar(out=st[:, 5:6], in0=st[:, 5:6], scalar1=float(EPS),
                                                   scalar2=None, op0=ALU.add),
               reads=[(tag, 5)], writes=[(tag, 5)])
        ctx.op("scalar", lambda s: s.activation(out=st[:, 6:7], in_=st[:, 5:6], func=AF.Ln),
               reads=[(tag, 5)], writes=[(tag, 6)])
        ctx.op("scalar", lambda s: s.activation(out=st[:, 6:7], in_=st[:, 6:7], func=AF.Exp, scale=-0.5),
               reads=[(tag, 6)], writes=[(tag, 6)])
        ctx.op("vector", lambda v: v.tensor_tensor(out=st[:, 7:8], in0=st[:, 2:3], in1=st[:, 6:7], op=ALU.mult),
               reads=[(tag, 2), (tag, 6)], writes=[(tag, 7)])
        ctx.op("vector", lambda v: v.tensor_scalar(out=st[:, 7:8], in0=st[:, 7:8], scalar1=-1.0,
                                                   scalar2=None, op0=ALU.mult),
               reads=[(tag, 7)], writes=[(tag, 7)])
        ctx.op("vector", lambda v: v.tensor_scalar(out=xb_out, in0=xin, scalar1=st[:, 6:7],
                                                   scalar2=st[:, 7:8], op0=ALU.mult, op1=ALU.add),
               reads=rk + [(tag, 6), (tag, 7)], writes=[wk])

    def ln_prologue(n_tiles, src_fn, src_keys, dst_fn, dkey, tag):
        ps = ExitStack()
        xbuf = [ps.enter_context(nc.sbuf_tensor(f"{tag}_x{i}", [P, D], F32)) for i in range(2)]
        dmp = ps.enter_context(nc.sbuf_tensor(f"{tag}_dmp", [P, D], F32))
        xb = [ps.enter_context(nc.sbuf_tensor(f"{tag}_xb{i}", [P, D], BF16)) for i in range(2)]
        stt = [ps.enter_context(nc.sbuf_tensor(f"{tag}_st{i}", [P, 8], F32)) for i in range(2)]
        ptr = [ps.enter_context(nc.psum_tensor(f"{tag}_ps{i}", [P, P], BF16)) for i in range(4)]
        for tb in range(n_tiles):
            sl = tb % 2
            ctx.dma("sync", xbuf[sl][:], src_fn(tb), reads=src_keys(tb), writes=[(tag + "x", sl)])
            ln_rows(xbuf[sl][:], stt[sl], xb[sl][:], [(tag + "x", sl)], (tag + "xb", sl), tag + "s" + str(sl),
                    dmp[:], (tag + "dmp",))
            for d in range(KD):
                j = d % 4
                ctx.op("tensor", lambda t, sl=sl, j=j, d=d:
                       t.transpose(out=ptr[j][:], in_=xb[sl][:, d * P:(d + 1) * P], identity=identb_sb[:]),
                       reads=[(tag + "xb", sl), ("identb",)], writes=[(tag + "ps", j)])
                if d % 2 == 0:
                    ctx.op("scalar", lambda s, j=j, d=d, tb=tb: s.copy(out=dst_fn(d, tb), in_=ptr[j][:]),
                           reads=[(tag + "ps", j)], writes=[(dkey, d)])
                else:
                    ctx.op("vector", lambda v, j=j, d=d, tb=tb: v.tensor_copy(out=dst_fn(d, tb), in_=ptr[j][:]),
                           reads=[(tag + "ps", j)], writes=[(dkey, d)])
        ctx.barrier()
        ps.close()

    def dump_scr_x(nm):
        if nm not in dump_aps:
            return
        for tb in range(NTT):
            ctx.dma("gpsimd", dump_aps[nm][tb * P:(tb + 1) * P, :], scr_x[tb * P:(tb + 1) * P, :],
                    reads=[("scr_x", tb)], writes=[("dump", nm, tb)])

    def phase1():
        ln_prologue(NTB, lambda tb: xkv[tb * P:(tb + 1) * P, :], lambda tb: [],
                    lambda d, tb: ar["a"][:, d * TKV + tb * P: d * TKV + (tb + 1) * P], "A", "p1")

        psA = ExitStack()
        ctn = psA.enter_context(nc.sbuf_tensor("c_ctn", [P, NTB * HA], F32))
        ps = ExitStack()
        CW = min(TKV, 1024)
        NW = TKV // CW
        ca = ps.enter_context(nc.sbuf_tensor("c_a", [HA, CW], F32))
        cb = ps.enter_context(nc.sbuf_tensor("c_b", [HA, CW], F32))
        cd = ps.enter_context(nc.sbuf_tensor("c_d", [HA, CW], F32))
        ce = ps.enter_context(nc.sbuf_tensor("c_e", [HA, CW], F32))
        conev = ps.enter_context(nc.sbuf_tensor("c_ones", [HA, CW], F32))
        c8b = ps.enter_context(nc.sbuf_tensor("c_8b", [HA, 3 * CW], BF16))
        carry = ps.enter_context(nc.sbuf_tensor("c_carry", [HA, 1], F32))
        wfrow = ps.enter_context(nc.sbuf_tensor("c_wfrow", [HA, D], F32))
        wfrowb = ps.enter_context(nc.sbuf_tensor("c_wfrowb", [HA, D], BF16))
        wft = ps.enter_context(nc.sbuf_tensor("c_wft", [P, KD * HA], BF16))
        bfsb = ps.enter_context(nc.sbuf_tensor("c_bf", [HA, 1], F32))
        mpref_sb = ps.enter_context(nc.sbuf_tensor("c_mpref", [P, NTB * HA], F32))
        psz = ps.enter_context(nc.psum_tensor("c_psz", [HA, CH], F32))
        pst = ps.enter_context(nc.psum_tensor("c_pst", [P, HA], F32))
        pstb = ps.enter_context(nc.psum_tensor("c_pstb", [P, HA], BF16))

        ctx.dma("sync", wfrow[:], wf[:], writes=[("wfrow",)])
        ctx.dma("sync", bfsb[:], bfv[:], writes=[("bfsb",)])
        ctx.dma("sync", mpref_sb[:], mpref[:], writes=[("mpref",)])
        ctx.op("scalar", lambda s: s.copy(out=wfrowb[:], in_=wfrow[:]), reads=[("wfrow",)], writes=[("wfrowb",)])
        for k in range(KD):
            ctx.op("tensor", lambda t, k=k: t.transpose(out=pstb[:], in_=wfrowb[:, k * P:(k + 1) * P],
                                                        identity=identb_sb[0:HA, 0:HA]),
                   reads=[("wfrowb",), ("identb",)], writes=[("c_pstb",)])
            ctx.op("scalar", lambda s, k=k: s.copy(out=wft[:, k * HA:(k + 1) * HA], in_=pstb[:]),
                   reads=[("c_pstb",)], writes=[("wft", k)])
        ctx.op("vector", lambda v: v.memset(conev[:], 1.0), writes=[("cones",)])
        ctx.op("vector", lambda v: v.memset(carry[:], 0.0), writes=[("ccar",)])
        kA, kB, kD_, kE = ("cwA",), ("cwB",), ("cwD",), ("cwE",)
        for w in range(NW):
            g0 = w * CW
            for ch in range(CW // CH):
                for k in range(KD):
                    ctx.op("tensor", lambda t, g0=g0, ch=ch, k=k:
                           t.matmul(out=psz[:], lhsT=wft[:, k * HA:(k + 1) * HA],
                                    rhs=Akv(k, g0 + ch * CH, g0 + (ch + 1) * CH),
                                    start=(k == 0), stop=(k == KD - 1)),
                           reads=[("wft", k), ("A", k)], writes=[("c_psz",)])
                ctx.op("vector", lambda v, ch=ch: v.tensor_scalar(out=ca[:, ch * CH:(ch + 1) * CH], in0=psz[:],
                                                                  scalar1=bfsb[:], scalar2=None, op0=ALU.add),
                       reads=[("c_psz",), ("bfsb",)], writes=[kA])
            ctx.op("scalar", lambda s: s.activation(out=cb[:], in_=ca[:], func=AF.Abs), reads=[kA], writes=[kB])
            ctx.op("scalar", lambda s: s.activation(out=cb[:], in_=cb[:], func=AF.Exp, scale=-1.0),
                   reads=[kB], writes=[kB])
            ctx.op("vector", lambda v: v.tensor_scalar(out=cb[:], in0=cb[:], scalar1=1.0, scalar2=None,
                                                       op0=ALU.add),
                   reads=[kB], writes=[kB])
            ctx.op("scalar", lambda s: s.activation(out=cb[:], in_=cb[:], func=AF.Ln), reads=[kB], writes=[kB])
            ctx.op("vector", lambda v: v.tensor_scalar_min(out=cd[:], in0=ca[:], scalar1=0.0),
                   reads=[kA], writes=[kD_])
            ctx.op("vector", lambda v: v.tensor_tensor(out=ca[:], in0=cd[:], in1=cb[:], op=ALU.subtract),
                   reads=[kB, kD_], writes=[kA])
            ctx.op("vector", lambda v: v.tensor_tensor_scan(out=ce[:], data0=conev[:], data1=ca[:],
                                                            initial=carry[:], op0=ALU.mult, op1=ALU.add),
                   reads=[("cones",), kA, ("ccar",)], writes=[kE])
            ctx.op("vector", lambda v: v.tensor_copy(out=carry[:], in_=ce[:, CW - 1:CW]),
                   reads=[kE], writes=[("ccar",)])
            ctx.op("vector", lambda v: v.tensor_scalar(out=ca[:], in0=ce[:], scalar1=8.0, scalar2=None,
                                                       op0=ALU.mult),
                   reads=[kE], writes=[kA])
            ctx.op("scalar", lambda s: s.copy(out=c8b[:, 0:CW], in_=ca[:]), reads=[kA], writes=[("c8b", 0)])
            ctx.op("vector", lambda v: v.tensor_copy(out=cb[:], in_=c8b[:, 0:CW]), reads=[("c8b", 0)], writes=[kB])
            ctx.op("vector", lambda v: v.tensor_tensor(out=cd[:], in0=ca[:], in1=cb[:], op=ALU.subtract),
                   reads=[kA, kB], writes=[kD_])
            ctx.op("scalar", lambda s: s.copy(out=c8b[:, CW:2 * CW], in_=cd[:]), reads=[kD_], writes=[("c8b", 1)])
            ctx.op("vector", lambda v: v.tensor_copy(out=cb[:], in_=c8b[:, CW:2 * CW]),
                   reads=[("c8b", 1)], writes=[kB])
            ctx.op("vector", lambda v: v.tensor_tensor(out=cd[:], in0=cd[:], in1=cb[:], op=ALU.subtract),
                   reads=[kD_, kB], writes=[kD_])
            ctx.op("scalar", lambda s: s.copy(out=c8b[:, 2 * CW:3 * CW], in_=cd[:]), reads=[kD_], writes=[("c8b", 2)])
            a0 = max(TGAP, g0); a1 = min(TKV, g0 + CW)
            if a1 > a0:
                for h in range(HA):
                    for si in range(3):
                        ctx.dma("gpsimd", scr_q[h * 67 + 64 + si:h * 67 + 65 + si, a0 - TGAP:a1 - TGAP],
                                c8b[h:h + 1, si * CW + (a0 - g0):si * CW + (a1 - g0)],
                                reads=[("c8b", si)], writes=[("scrq_aug", h, si)])
            ctx.op("vector", lambda v: v.tensor_scalar(out=ce[:], in0=ce[:], scalar1=-1.0, scalar2=None,
                                                       op0=ALU.mult),
                   reads=[kE], writes=[kE])
            for tbl in range(CW // P):
                tb = g0 // P + tbl
                ctx.op("tensor", lambda t, tbl=tbl: t.transpose(out=pst[:], in_=ce[:, tbl * P:(tbl + 1) * P],
                                                                identity=identf_sb[0:HA, 0:HA]),
                       reads=[kE, ("identf",)], writes=[("c_pst",)])
                ctx.op("scalar", lambda s, tb=tb: s.copy(out=ctn[:, tb * HA:(tb + 1) * HA], in_=pst[:]),
                       reads=[("c_pst",)], writes=[("ctn", tb)])
                ctx.op("vector", lambda v, tb=tb: v.tensor_tensor(out=ctn[:, tb * HA:(tb + 1) * HA],
                                                                  in0=ctn[:, tb * HA:(tb + 1) * HA],
                                                                  in1=mpref_sb[:, tb * HA:(tb + 1) * HA],
                                                                  op=ALU.add),
                       reads=[("mpref",), ("ctn", tb)], writes=[("ctn", tb)])
        ctx.barrier()
        ps.close()

        ps2 = ExitStack()
        qwt = [ps2.enter_context(nc.sbuf_tensor(f"pr_qwt{i}", [P, KD * P], BF16)) for i in range(2)]
        pev = [ps2.enter_context(nc.psum_tensor(f"pr_ps{i}", [P, CH], F32)) for i in range(2)]
        qev = [ps2.enter_context(nc.sbuf_tensor(f"pr_ev{i}", [P, CH], BF16)) for i in range(2)]

        for nm, dst, nch, off, rows in (("wqa", scr_q, NCO, TGAP, 67), ("wka", scr_k, NCK, 0, 64)):
            for dot in range(KD):
                wsl = dot % 2
                for k in range(KD):
                    ctx.dma("sync", qwt[wsl][:, k * P:(k + 1) * P],
                            wT[nm][(dot * KD + k) * P:(dot * KD + k + 1) * P, :],
                            reads=[("wT" + nm, dot, k)], writes=[("qwt", wsl, k)])
                for ch in range(nch):
                    ssl = (dot * nch + ch) % 2
                    for k in range(KD):
                        ctx.op("tensor", lambda t, wsl=wsl, ssl=ssl, ch=ch, k=k, off=off:
                               t.matmul(out=pev[ssl][:], lhsT=qwt[wsl][:, k * P:(k + 1) * P],
                                        rhs=Akv(k, off + ch * CH, off + (ch + 1) * CH),
                                        start=(k == 0), stop=(k == KD - 1)),
                               reads=[("qwt", wsl, k), ("A", k)], writes=[("pr_ps", ssl)])
                    ctx.op("scalar", lambda s, ssl=ssl: s.copy(out=qev[ssl][:], in_=pev[ssl][:]),
                           reads=[("pr_ps", ssl)], writes=[("pr_ev", ssl)])
                    for half in range(2):
                        h = dot * 2 + half
                        ctx.dma("gpsimd",
                                dst[h * rows:h * rows + 64, ch * CH:(ch + 1) * CH],
                                qev[ssl][half * 64:half * 64 + 64, :],
                                reads=[("pr_ev", ssl)], writes=[("scrp" + nm, h, ch)])
        vwt = [ps2.enter_context(nc.sbuf_tensor(f"pr_vwt{i}", [P, KD * CH], BF16)) for i in range(2)]
        for dvc in range(DCH):
            wsl = dvc % 2
            for k in range(KD):
                for j in range(SLOTS):
                    dot = dvc * SLOTS + j
                    ctx.dma("sync", vwt[wsl][:, k * CH + j * P:k * CH + (j + 1) * P],
                            wT["wva"][(dot * KD + k) * P:(dot * KD + k + 1) * P, :],
                            reads=[("wTwva", dot, k)], writes=[("vwt", wsl, k, j)])
            for tb in range(NTB):
                ssl = (dvc * NTB + tb) % 2
                for k in range(KD):
                    ctx.op("tensor", lambda t, wsl=wsl, ssl=ssl, tb=tb, k=k:
                           t.matmul(out=pev[ssl][:], lhsT=Akv(k, tb * P, (tb + 1) * P),
                                    rhs=vwt[wsl][:, k * CH:(k + 1) * CH],
                                    start=(k == 0), stop=(k == KD - 1)),
                           reads=[("vwt", wsl, k, j) for j in range(SLOTS)] + [("A", k)],
                           writes=[("pr_ps", ssl)])
                ctx.op("scalar", lambda s, ssl=ssl: s.copy(out=qev[ssl][:], in_=pev[ssl][:]),
                       reads=[("pr_ps", ssl)], writes=[("pr_ev", ssl)])
                ctx.dma("gpsimd", scr_v[tb * P:(tb + 1) * P, dvc * CH:(dvc + 1) * CH], qev[ssl][:],
                        reads=[("pr_ev", ssl)], writes=[("scr_v", tb, dvc)])
        ctx.barrier()
        ps2.close()

        ps3 = ExitStack()
        qh = [ps3.enter_context(nc.sbuf_tensor(f"at_qh{i}", [67, TOWN], BF16)) for i in range(2)]
        kh = [ps3.enter_context(nc.sbuf_tensor(f"at_kh{i}", [67, P], BF16)) for i in range(2)]
        vh = [ps3.enter_context(nc.sbuf_tensor(f"at_vh{i}", [P, 65], BF16)) for i in range(2)]
        ptb = [ps3.enter_context(nc.sbuf_tensor(f"at_pt{i}", [P, CH], BF16)) for i in range(2)]
        psS = [ps3.enter_context(nc.psum_tensor(f"at_psS{i}", [P, CH], F32)) for i in range(2)]
        psAV = [ps3.enter_context(nc.psum_tensor(f"at_psAV{i}", [65, CH], F32)) for i in range(NCO)]
        psB = ps3.enter_context(nc.psum_tensor("at_psB", [64, CH], F32))
        avsb = ps3.enter_context(nc.sbuf_tensor("at_avsb", [64, CH], BF16))
        lrow = ps3.enter_context(nc.sbuf_tensor("at_lrow", [1, CH], F32))
        linv = ps3.enter_context(nc.sbuf_tensor("at_linv", [1, CH], F32))
        for i in range(2):
            ctx.op("vector", lambda v, i=i: v.memset(kh[i][64:67, :], 1.0), writes=[("khones", i)])
            ctx.op("vector", lambda v, i=i: v.memset(vh[i][:, 64:65], 1.0), writes=[("vhones", i)])

        for h in range(HA):
            hs = h % 2
            ctx.dma("sync", qh[hs][:], scr_q[h * 67:(h + 1) * 67, :],
                    reads=[("scrpwqa", h, ch) for ch in range(NCO)] +
                          [("scrq_aug", h, si) for si in range(3)],
                    writes=[("qh", hs)])
            for jb in range(NTB):
                bs = (h * NTB + jb) % 2
                ctx.dma("sync", kh[bs][0:64, :], scr_k[h * 64:h * 64 + 64, jb * P:(jb + 1) * P],
                        reads=[("scrpwka", h, jb * P // CH)], writes=[("kh", bs)])
                ctx.dma("sync", vh[bs][:, 0:64], scr_v[jb * P:(jb + 1) * P, h * 64:h * 64 + 64],
                        reads=[("scr_v", jb, (h * 64) // CH)], writes=[("vh", bs)])
                for ic in range(NCO):
                    if jb > jbmax(ic):
                        continue
                    ss = (h * NTB * NCO + jb * NCO + ic) % 2
                    ctx.op("tensor", lambda t, bs=bs, hs=hs, ss=ss, ic=ic:
                           t.matmul(out=psS[ss][:], lhsT=kh[bs][:], rhs=qh[hs][:, ic * CH:(ic + 1) * CH],
                                    start=True, stop=True),
                           reads=[("kh", bs), ("khones", bs), ("qh", hs)], writes=[("psS", ss)])
                    if bandst(ic) <= jb <= jbmax(ic):
                        slot = jb - bandst(ic)
                        ctx.op("vector", lambda v, ss=ss, slot=slot:
                               v.tensor_tensor(out=psS[ss][:], in0=psS[ss][:],
                                               in1=mtri_sb[:, slot * CH:(slot + 1) * CH], op=ALU.add),
                               reads=[("psS", ss), ("mtri", slot)], writes=[("psS", ss)])
                    ctx.op("scalar", lambda s, ss=ss, jb=jb, h=h:
                           s.activation(out=ptb[ss][:], in_=psS[ss][:], func=AF.Exp, scale=0.125,
                                        bias=ctn[:, jb * HA + h:jb * HA + h + 1]),
                           reads=[("psS", ss), ("ctn", jb)], writes=[("ptb", ss)])
                    ctx.op("tensor", lambda t, bs=bs, ic=ic, jb=jb, ss=ss:
                           t.matmul(out=psAV[ic][:], lhsT=vh[bs][:], rhs=ptb[ss][:],
                                    start=(jb == 0), stop=(jb == jbmax(ic))),
                           reads=[("vh", bs), ("vhones", bs), ("ptb", ss)], writes=[("psAV", ic)])
            for ic in range(NCO):
                ctx.op("scalar", lambda s, ic=ic: s.copy(out=lrow[:], in_=psAV[ic][64:65, :]),
                       reads=[("psAV", ic)], writes=[("lrow",)])
                ctx.op("scalar", lambda s, ic=ic: s.copy(out=avsb[:], in_=psAV[ic][0:64, :]),
                       reads=[("psAV", ic)], writes=[("avsb",)])
                ctx.op("vector", lambda v: v.reciprocal(out=linv[:], in_=lrow[:]),
                       reads=[("lrow",)], writes=[("linv",)])
                ctx.op("tensor", lambda t: t.matmul(out=psB[:], lhsT=onesf[:, 0:64], rhs=linv[:],
                                                    start=True, stop=True),
                       reads=[("onesf",), ("linv",)], writes=[("psB",)])
                r0 = (h % 2) * 64
                dcol = (h // 2) * TOWN + ic * CH
                ctx.op("vector", lambda v, r0=r0, dcol=dcol:
                       v.tensor_tensor(out=Bv(dcol, dcol + CH, r0, r0 + 64), in0=avsb[:], in1=psB[:],
                                       op=ALU.mult),
                       reads=[("avsb",), ("psB",)], writes=[("B", h // 2, h % 2)])
        ctx.barrier()
        ps3.close()
        psA.close()

        ps4 = ExitStack()
        for k in range(KD):
            for do in range(KD):
                ctx.dma("sync", Wr1(k, do * P, (do + 1) * P),
                        wT["woa"][(do * KD + k) * P:(do * KD + k + 1) * P, :],
                        reads=[("wTwoa", do, k)], writes=[("W1", k, do)])
        psO = [ps4.enter_context(nc.psum_tensor(f"wo_ps{i}", [P, CH], F32)) for i in range(2)]
        xob = [ps4.enter_context(nc.sbuf_tensor(f"wo_xo{i}", [P, CH], F32)) for i in range(2)]
        x1b = [ps4.enter_context(nc.sbuf_tensor(f"wo_x1{i}", [P, CH], F32)) for i in range(2)]
        for tt in range(NTT):
            for dc in range(DCH):
                ss = (tt * DCH + dc) % 2
                for k in range(KD):
                    ctx.op("tensor", lambda t, ss=ss, k=k, tt=tt, dc=dc:
                           t.matmul(out=psO[ss][:], lhsT=Bv(k * TOWN + tt * P, k * TOWN + (tt + 1) * P),
                                    rhs=Wr1(k, dc * CH, (dc + 1) * CH),
                                    start=(k == 0), stop=(k == KD - 1)),
                           reads=[("B", k, 0), ("B", k, 1)] +
                                 [("W1", k, dc * SLOTS + j) for j in range(SLOTS)],
                           writes=[("psO", ss)])
                ctx.dma("sync", xob[ss][:], xown[tt * P:(tt + 1) * P, dc * CH:(dc + 1) * CH],
                        writes=[("xob", ss)])
                ctx.op("vector", lambda v, ss=ss: v.tensor_tensor(out=x1b[ss][:], in0=psO[ss][:], in1=xob[ss][:],
                                                                  op=ALU.add),
                       reads=[("psO", ss), ("xob", ss)], writes=[("x1b", ss)])
                ctx.dma("gpsimd", scr_x[tt * P:(tt + 1) * P, dc * CH:(dc + 1) * CH], x1b[ss][:],
                        reads=[("x1b", ss)], writes=[("scr_x", tt)])
        ctx.barrier()
        ps4.close()

    def ffn_phase(tagp, final):
        ln_prologue(NTT, lambda tb: scr_x[tb * P:(tb + 1) * P, :], lambda tb: [("scr_x", tb)],
                    lambda d, tb: Ao(d, tb * P, (tb + 1) * P), "A", tagp)
        ps = ExitStack()
        wgt = [ps.enter_context(nc.sbuf_tensor(f"{tagp}_wgt{i}", [P, KD * P], BF16)) for i in range(2)]
        wut = [ps.enter_context(nc.sbuf_tensor(f"{tagp}_wut{i}", [P, KD * P], BF16)) for i in range(2)]
        wdt = [ps.enter_context(nc.sbuf_tensor(f"{tagp}_wdt{i}", [P, CH], BF16)) for i in range(2)]
        psG = ps.enter_context(nc.psum_tensor(tagp + "_psG", [P, CH], F32))
        psU = ps.enter_context(nc.psum_tensor(tagp + "_psU", [P, CH], F32))
        gb = ps.enter_context(nc.sbuf_tensor(tagp + "_gb", [P, CH], BF16))
        ub = ps.enter_context(nc.sbuf_tensor(tagp + "_ub", [P, CH], BF16))
        psD = [ps.enter_context(nc.psum_tensor(f"{tagp}_psD{i}", [P, CH], F32)) for i in range(SLOTS)]
        xrb = [ps.enter_context(nc.sbuf_tensor(f"{tagp}_xr{i}", [P, CH], F32)) for i in range(2)]
        xnb = [ps.enter_context(nc.sbuf_tensor(f"{tagp}_xn{i}", [P, CH], F32)) for i in range(2)]
        dst = out if final else scr_x

        for ts in range(NCO):
            for ft in range(NF):
                wsl = ft % 2
                for k in range(KD):
                    ctx.dma("sync", wgt[wsl][:, k * P:(k + 1) * P],
                            scr_wg[(ft * KD + k) * P:(ft * KD + k + 1) * P, :],
                            reads=[("scrw_wg", k, ft)], writes=[("wgt", wsl, k)])
                    ctx.dma("scalar", wut[wsl][:, k * P:(k + 1) * P],
                            scr_wu[(ft * KD + k) * P:(ft * KD + k + 1) * P, :],
                            reads=[("scrw_wu", k, ft)], writes=[("wut", wsl, k)])
                for k in range(KD):
                    ctx.op("tensor", lambda t, wsl=wsl, k=k, ts=ts:
                           t.matmul(out=psG[:], lhsT=wgt[wsl][:, k * P:(k + 1) * P],
                                    rhs=Ao(k, ts * CH, (ts + 1) * CH), start=(k == 0), stop=(k == KD - 1)),
                           reads=[("wgt", wsl, k), ("A", k)], writes=[("psG",)])
                for k in range(KD):
                    ctx.op("tensor", lambda t, wsl=wsl, k=k, ts=ts:
                           t.matmul(out=psU[:], lhsT=wut[wsl][:, k * P:(k + 1) * P],
                                    rhs=Ao(k, ts * CH, (ts + 1) * CH), start=(k == 0), stop=(k == KD - 1)),
                           reads=[("wut", wsl, k), ("A", k)], writes=[("psU",)])
                ctx.op("scalar", lambda s: s.activation(out=gb[:], in_=psG[:], func=AF.Silu),
                       reads=[("psG",)], writes=[("gb",)])
                ctx.op("scalar", lambda s: s.copy(out=ub[:], in_=psU[:]),
                       reads=[("psU",)], writes=[("ub",)])
                ctx.op("vector", lambda v, ft=ft: v.tensor_tensor(out=Bv(ft * CH, (ft + 1) * CH),
                                                                  in0=gb[:], in1=ub[:], op=ALU.mult),
                       reads=[("gb",), ("ub",)], writes=[("B", ft)])
            for hc in range(DCH):
                for ft in range(NF):
                    wsl = ft % 2
                    ctx.dma("sync", wdt[wsl][:], scr_wd[ft * P:(ft + 1) * P, hc * CH:(hc + 1) * CH],
                            reads=[("scrw_wd", ft)], writes=[("wdt", wsl)])
                    for tl in range(SLOTS):
                        ctx.op("tensor", lambda t, wsl=wsl, ft=ft, tl=tl:
                               t.matmul(out=psD[tl][:], lhsT=Bv(ft * CH + tl * P, ft * CH + (tl + 1) * P),
                                        rhs=wdt[wsl][:], start=(ft == 0), stop=(ft == NF - 1)),
                               reads=[("wdt", wsl), ("B", ft)], writes=[("psD", tl)])
                for tl in range(SLOTS):
                    rsl = (hc * SLOTS + tl) % 2
                    row0 = ts * CH + tl * P
                    ctx.dma("sync", xrb[rsl][:], scr_x[row0:row0 + P, hc * CH:(hc + 1) * CH],
                            reads=[("scr_x", row0 // P)], writes=[("xrb", rsl)])
                    ctx.op("vector", lambda v, rsl=rsl, tl=tl:
                           v.tensor_tensor(out=xnb[rsl][:], in0=psD[tl][:], in1=xrb[rsl][:], op=ALU.add),
                           reads=[("psD", tl), ("xrb", rsl)], writes=[("xnb", rsl)])
                    ctx.dma("gpsimd", dst[row0:row0 + P, hc * CH:(hc + 1) * CH], xnb[rsl][:],
                            reads=[("xnb", rsl)],
                            writes=[("scr_x", row0 // P) if not final else ("outw", row0 // P, hc)])
        ctx.barrier()
        ps.close()

    def phase3():
        ln_prologue(NTT, lambda tb: scr_x[tb * P:(tb + 1) * P, :], lambda tb: [("scr_x", tb)],
                    lambda d, tb: Ao(d, tb * P, (tb + 1) * P), "A", "p3")
        ps = ExitStack()
        psP = [ps.enter_context(nc.psum_tensor(f"p3_pp{i}", [P, CH], F32)) for i in range(2)]
        evb = [ps.enter_context(nc.sbuf_tensor(f"p3_ev{i}", [P, CH], BF16)) for i in range(2)]
        for k in range(KD):
            for do in range(KD):
                ctx.dma("sync", Wr2(k, do * P, (do + 1) * P),
                        wT["wgb"][(do * KD + k) * P:(do * KD + k + 1) * P, :],
                        reads=[("wTwgb", do, k)], writes=[("W3a", k, do)])
        for tt in range(NTT):
            for dc in range(DCH):
                ss = (tt * DCH + dc) % 2
                for k in range(KD):
                    ctx.op("tensor", lambda t, ss=ss, k=k, tt=tt, dc=dc:
                           t.matmul(out=psP[ss][:], lhsT=Ao(k, tt * P, (tt + 1) * P),
                                    rhs=Wr2(k, dc * CH, (dc + 1) * CH), start=(k == 0), stop=(k == KD - 1)),
                           reads=[("A", k)] + [("W3a", k, dc * SLOTS + j) for j in range(SLOTS)],
                           writes=[("p3ps", ss)])
                ctx.op("scalar", lambda s, ss=ss: s.activation(out=evb[ss][:], in_=psP[ss][:], func=AF.Silu),
                       reads=[("p3ps", ss)], writes=[("p3ev", ss)])
                ctx.dma("gpsimd", scr_g[tt * P:(tt + 1) * P, dc * CH:(dc + 1) * CH], evb[ss][:],
                        reads=[("p3ev", ss)], writes=[("scr_g", tt, dc)])
        qwt = [ps.enter_context(nc.sbuf_tensor(f"p3_qwt{i}", [P, KD * P], BF16)) for i in range(2)]
        for nm, dstt in (("wqb", scr_qb), ("wvb", scr_vb)):
            for dot in range(KD):
                wsl = dot % 2
                for k in range(KD):
                    ctx.dma("sync", qwt[wsl][:, k * P:(k + 1) * P],
                            wT[nm][(dot * KD + k) * P:(dot * KD + k + 1) * P, :],
                            reads=[("wT" + nm, dot, k)], writes=[("p3qwt", wsl, k)])
                for ch in range(NCO):
                    ss = (dot * NCO + ch) % 2
                    for k in range(KD):
                        ctx.op("tensor", lambda t, wsl=wsl, ss=ss, ch=ch, k=k:
                               t.matmul(out=psP[ss][:], lhsT=qwt[wsl][:, k * P:(k + 1) * P],
                                        rhs=Ao(k, ch * CH, (ch + 1) * CH), start=(k == 0), stop=(k == KD - 1)),
                               reads=[("p3qwt", wsl, k), ("A", k)], writes=[("p3ps", ss)])
                    ctx.op("scalar", lambda s, ss=ss: s.copy(out=evb[ss][:], in_=psP[ss][:]),
                           reads=[("p3ps", ss)], writes=[("p3ev", ss)])
                    ctx.dma("gpsimd", dstt[dot * P:(dot + 1) * P, ch * CH:(ch + 1) * CH], evb[ss][:],
                            reads=[("p3ev", ss)], writes=[("scrb" + nm, dot, ch)])
        ctx.barrier()
        ps.close()

        ps2 = ExitStack()
        betasb = ps2.enter_context(nc.sbuf_tensor("p3_beta", [1, HB], F32))
        lamr = ps2.enter_context(nc.sbuf_tensor("p3_lamr", [1, HB], F32))
        lamc = ps2.enter_context(nc.sbuf_tensor("p3_lamc", [P, HB], F32))
        psLC = ps2.enter_context(nc.psum_tensor("p3_psLC", [P, HB], F32))
        onest = ps2.enter_context(nc.sbuf_tensor("p3_onest", [P, TOWN], F32))
        lamt = ps2.enter_context(nc.sbuf_tensor("p3_lamt", [P, TOWN], F32))
        ret1 = ps2.enter_context(nc.sbuf_tensor("p3_ret", [P, TOWN], F32))
        csend = ps2.enter_context(nc.sbuf_tensor("p3_csend", [P, HB], F32))
        crecv = ps2.enter_context(nc.sbuf_tensor("p3_crecv", [P, HB], F32))
        cuse = ps2.enter_context(nc.sbuf_tensor("p3_cuse", [P, HB], F32))
        qbh = [ps2.enter_context(nc.sbuf_tensor(f"p3_qbh{i}", [P, TOWN], BF16)) for i in range(2)]
        oth = [ps2.enter_context(nc.sbuf_tensor(f"p3_oth{i}", [P, TOWN], BF16)) for i in range(2)]
        ptr3 = [ps2.enter_context(nc.psum_tensor(f"p3_ptr{i}", [P, P], BF16)) for i in range(4)]

        ctx.dma("sync", betasb[:], betab[:], writes=[("beta",)])
        ctx.op("scalar", lambda s: s.activation(out=lamr[:], in_=betasb[:], func=AF.Exp, scale=-LN2),
               reads=[("beta",)], writes=[("lamr",)])
        ctx.op("vector", lambda v: v.tensor_scalar(out=lamr[:], in0=lamr[:], scalar1=-1.0, scalar2=1.0,
                                                   op0=ALU.mult, op1=ALU.add),
               reads=[("lamr",)], writes=[("lamr",)])
        ctx.op("vector", lambda v: v.tensor_scalar_max(out=lamr[:], in0=lamr[:], scalar1=1.1754944e-38),
               reads=[("lamr",)], writes=[("lamr",)])
        ctx.op("vector", lambda v: v.tensor_scalar_min(out=lamr[:], in0=lamr[:], scalar1=1.0 - 1e-9),
               reads=[("lamr",)], writes=[("lamr",)])
        ctx.op("tensor", lambda t: t.matmul(out=psLC[:], lhsT=onesf[:], rhs=lamr[:], start=True, stop=True),
               reads=[("onesf",), ("lamr",)], writes=[("psLC",)])
        ctx.op("scalar", lambda s: s.copy(out=lamc[:], in_=psLC[:]), reads=[("psLC",)], writes=[("lamc",)])
        ctx.op("vector", lambda v: v.memset(onest[:], 1.0), writes=[("onest",)])
        for h in range(HB):
            ctx.dma("sync", Bv(h * TOWN, (h + 1) * TOWN), scr_vb[h * P:(h + 1) * P, :],
                    reads=[("scrbwvb", h, ch) for ch in range(NCO)], writes=[("B", h)])
            ctx.op("vector", lambda v, h=h: v.tensor_scalar(out=lamt[:], in0=onest[:],
                                                            scalar1=lamc[:, h:h + 1], scalar2=None, op0=ALU.mult),
                   reads=[("onest",), ("lamc",)], writes=[("lamt",)])
            ctx.op("vector", lambda v, h=h: v.tensor_tensor_scan(out=ret1[:], data0=lamt[:],
                                                                 data1=Bv(h * TOWN, (h + 1) * TOWN),
                                                                 initial=0.0, op0=ALU.mult, op1=ALU.add),
                   reads=[("lamt",), ("B", h)], writes=[("ret1",)])
            ctx.op("vector", lambda v, h=h: v.tensor_tensor(out=csend[:, h:h + 1], in0=ret1[:, TOWN - 1:TOWN],
                                                            in1=rolenv_sb[:], op=ALU.mult),
                   reads=[("ret1",), ("rolenv",)], writes=[("csend", h)])
        swidx = ps2.enter_context(nc.sbuf_tensor("x_widx", [P, 1], mybir.dt.int32))
        sridx = ps2.enter_context(nc.sbuf_tensor("x_ridx", [P, 1], mybir.dt.int32))
        snon = ps2.enter_context(nc.sbuf_tensor("x_non", [P, 16], mybir.dt.int32))
        fpoll = ps2.enter_context(nc.sbuf_tensor("x_fp", [P, 16], mybir.dt.int32))
        crecvr = ps2.enter_context(nc.sbuf_tensor("x_cr", [P, HB], F32))
        ctx.dma("sync", swidx[:], widx[:], writes=[("swidx",)])
        ctx.dma("sync", sridx[:], ridx[:], writes=[("sridx",)])
        ctx.dma("sync", snon[:], noncev[:], writes=[("snon",)])
        ctx.op("gpsimd", lambda g: g.engine_nop(),
               reads=[("csend", h) for h in range(HB)] + [("swidx",), ("sridx",), ("snon",)],
               writes=[])
        IOA = bass.IndirectOffsetOnAxis
        xsem = ctx.esem

        def xchg(g):
            g.indirect_dma_start(out=shc[:], out_offset=IOA(ap=swidx[:], axis=0), in_=csend[:],
                                 in_offset=None, oob_is_err=False).then_inc(ccsem, 16)
            g.wait_ge(ccsem, 16)
            g.indirect_dma_start(out=shf[:], out_offset=IOA(ap=swidx[:], axis=0), in_=snon[:],
                                 in_offset=None, oob_is_err=False).then_inc(ccsem, 16)
            g.wait_ge(ccsem, 32)
            with g.register("xr") as xr, g.register("xr2") as xr2, g.register("xr3") as xr3, \
                    g.register("xthr") as xthr:
                g.reg_load(xr3, snon[0:1, 0:1])
                g.reg_mov(xr, 1)
                g.reg_mov(xthr, 32)
                with g.While(xr):
                    g.reg_add(xthr, xthr, 16)
                    g.indirect_dma_start(out=fpoll[:], out_offset=None, in_=shf[:],
                                         in_offset=IOA(ap=sridx[:], axis=0),
                                         oob_is_err=False).then_inc(ccsem, 16)
                    g.wait_ge(ccsem, xthr)
                    g.reg_load(xr2, fpoll[0:1, 0:1])
                    g.reg_sub(xr2, xr2, xr3)
                    with g.If_eq(xr2, 0):
                        g.reg_mov(xr, 0)
                    with g.Else():
                        g.reg_mov(xr, 1)
                g.reg_add(xthr, xthr, 16)
                g.indirect_dma_start(out=crecvr[:], out_offset=None, in_=shc[:],
                                     in_offset=IOA(ap=sridx[:], axis=0), oob_is_err=False).then_inc(ccsem, 16)
                g.wait_ge(ccsem, xthr)

        ctx.raw("gpsimd", xchg)
        ctx.op("gpsimd", lambda g: g.tensor_copy(out=crecv[:], in_=crecvr[:]),
               reads=[], writes=[("crecv",)])
        ctx.op("vector", lambda v: v.tensor_scalar(out=cuse[:], in0=crecv[:], scalar1=rolev_sb[:],
                                                   scalar2=None, op0=ALU.mult),
               reads=[("crecv",), ("rolev",)], writes=[("cuse",)])

        for h in range(HB):
            hs = h % 2
            ctx.dma("sync", qbh[hs][:], scr_qb[h * P:(h + 1) * P, :],
                    reads=[("scrbwqb", h, ch) for ch in range(NCO)], writes=[("qbh", hs)])
            ctx.op("vector", lambda v, h=h: v.tensor_scalar(out=lamt[:], in0=onest[:],
                                                            scalar1=lamc[:, h:h + 1], scalar2=None, op0=ALU.mult),
                   reads=[("onest",), ("lamc",)], writes=[("lamt",)])
            ctx.op("vector", lambda v, h=h: v.tensor_tensor_scan(out=ret1[:], data0=lamt[:],
                                                                 data1=Bv(h * TOWN, (h + 1) * TOWN),
                                                                 initial=cuse[:, h:h + 1], op0=ALU.mult,
                                                                 op1=ALU.add),
                   reads=[("lamt",), ("B", h), ("cuse",)], writes=[("ret1",)])
            ctx.op("vector", lambda v, hs=hs: v.tensor_tensor(out=oth[hs][:], in0=qbh[hs][:], in1=ret1[:],
                                                              op=ALU.mult),
                   reads=[("qbh", hs), ("ret1",)], writes=[("oth", hs)])
            for tb in range(NTT):
                j = tb % 4
                ctx.op("tensor", lambda t, hs=hs, j=j, tb=tb:
                       t.transpose(out=ptr3[j][:], in_=oth[hs][:, tb * P:(tb + 1) * P], identity=identb_sb[:]),
                       reads=[("oth", hs), ("identb",)], writes=[("p3tr", j)])
                if tb % 2 == 0:
                    ctx.op("scalar", lambda s, j=j, tb=tb, h=h:
                           s.copy(out=ar["a"][:, tb * D + h * P:tb * D + (h + 1) * P], in_=ptr3[j][:]),
                           reads=[("p3tr", j)], writes=[("A3", tb)])
                else:
                    ctx.op("vector", lambda v, j=j, tb=tb, h=h:
                           v.tensor_copy(out=ar["a"][:, tb * D + h * P:tb * D + (h + 1) * P], in_=ptr3[j][:]),
                           reads=[("p3tr", j)], writes=[("A3", tb)])
        ctx.barrier()
        ps2.close()

        ps3 = ExitStack()
        gamb = ps3.enter_context(nc.sbuf_tensor("p3_gamb", [P, D], F32))
        betb = ps3.enter_context(nc.sbuf_tensor("p3_betb", [P, D], F32))
        grow = ps3.enter_context(nc.sbuf_tensor("p3_grow", [1, D], F32))
        st3 = ps3.enter_context(nc.sbuf_tensor("p3_st", [P, 8], F32))
        nnb = ps3.enter_context(nc.sbuf_tensor("p3_nnb", [P, D], F32))
        gt3 = [ps3.enter_context(nc.sbuf_tensor(f"p3_gt{i}", [P, D], BF16)) for i in range(2)]
        zgb = ps3.enter_context(nc.sbuf_tensor("p3_zg", [P, D], BF16))
        ztl = [ps3.enter_context(nc.sbuf_tensor(f"p3_ztl{i}", [P, KD * P], BF16)) for i in range(2)]
        xrb3 = [ps3.enter_context(nc.sbuf_tensor(f"p3_xr{i}", [P, CH], F32)) for i in range(2)]
        x3b = [ps3.enter_context(nc.sbuf_tensor(f"p3_x3{i}", [P, CH], F32)) for i in range(2)]
        ptr4 = [ps3.enter_context(nc.psum_tensor(f"p3_ptr4{i}", [P, P], BF16)) for i in range(4)]
        psP2 = [ps3.enter_context(nc.psum_tensor(f"p3_psP2{i}", [P, CH], F32)) for i in range(2)]

        for k in range(KD):
            for do in range(KD):
                ctx.dma("sync", Wr2(k, do * P, (do + 1) * P),
                        wT["wob"][(do * KD + k) * P:(do * KD + k + 1) * P, :],
                        reads=[("wTwob", do, k)], writes=[("W3f", k, do)])
        for nm, src, dstb in (("lng", lng, gamb), ("lnb", lnb, betb)):
            ctx.dma("sync", grow[:], src[:], writes=[("grow",)])
            for c in range(DCH):
                ctx.op("tensor", lambda t, c=c: t.matmul(out=psP2[0][:], lhsT=onesf[:],
                                                         rhs=grow[0:1, c * CH:(c + 1) * CH],
                                                         start=True, stop=True),
                       reads=[("grow",), ("onesf",)], writes=[("p3ps2", 0)])
                ctx.op("scalar", lambda s, c=c, dstb=dstb: s.copy(out=dstb[:, c * CH:(c + 1) * CH],
                                                                  in_=psP2[0][:]),
                       reads=[("p3ps2", 0)], writes=[(nm, c)])
        gamkeys = [("lng", c) for c in range(DCH)]
        betkeys = [("lnb", c) for c in range(DCH)]
        for tt in range(NTT):
            hs = tt % 2
            otile = ar["a"][:, tt * D:(tt + 1) * D]
            ln_rows(otile, st3, nnb[:], [("A3", tt)], ("nnb",), "p3f", zgb[:], ("zgb",))
            ctx.op("vector", lambda v: v.tensor_tensor(out=nnb[:], in0=nnb[:], in1=gamb[:], op=ALU.mult),
                   reads=[("nnb",)] + gamkeys, writes=[("nnb",)])
            ctx.op("vector", lambda v: v.tensor_tensor(out=nnb[:], in0=nnb[:], in1=betb[:], op=ALU.add),
                   reads=[("nnb",)] + betkeys, writes=[("nnb",)])
            ctx.dma("sync", gt3[hs][:], scr_g[tt * P:(tt + 1) * P, :],
                    reads=[("scr_g", tt, dc) for dc in range(DCH)], writes=[("gt3", hs)])
            ctx.op("vector", lambda v, hs=hs: v.tensor_tensor(out=zgb[:], in0=nnb[:], in1=gt3[hs][:],
                                                              op=ALU.mult),
                   reads=[("nnb",), ("gt3", hs)], writes=[("zgb",)])
            for d in range(KD):
                j = d % 4
                ctx.op("tensor", lambda t, j=j, d=d:
                       t.transpose(out=ptr4[j][:], in_=zgb[:, d * P:(d + 1) * P], identity=identb_sb[:]),
                       reads=[("zgb",), ("identb",)], writes=[("p3tr4", j)])
                if d % 2 == 0:
                    ctx.op("scalar", lambda s, j=j, d=d, hs=hs: s.copy(out=ztl[hs][:, d * P:(d + 1) * P],
                                                                       in_=ptr4[j][:]),
                           reads=[("p3tr4", j)], writes=[("ztl", hs, d)])
                else:
                    ctx.op("vector", lambda v, j=j, d=d, hs=hs: v.tensor_copy(out=ztl[hs][:, d * P:(d + 1) * P],
                                                                              in_=ptr4[j][:]),
                           reads=[("p3tr4", j)], writes=[("ztl", hs, d)])
            for dc in range(DCH):
                ss = (tt * DCH + dc) % 2
                for k in range(KD):
                    ctx.op("tensor", lambda t, ss=ss, k=k, dc=dc, hs=hs:
                           t.matmul(out=psP2[ss][:], lhsT=ztl[hs][:, k * P:(k + 1) * P],
                                    rhs=Wr2(k, dc * CH, (dc + 1) * CH), start=(k == 0), stop=(k == KD - 1)),
                           reads=[("ztl", hs, k)] + [("W3f", k, dc * SLOTS + j) for j in range(SLOTS)],
                           writes=[("p3ps2", ss)])
                ctx.dma("sync", xrb3[ss][:], scr_x[tt * P:(tt + 1) * P, dc * CH:(dc + 1) * CH],
                        reads=[("scr_x", tt)], writes=[("xrb3", ss)])
                ctx.op("vector", lambda v, ss=ss: v.tensor_tensor(out=x3b[ss][:], in0=psP2[ss][:],
                                                                  in1=xrb3[ss][:], op=ALU.add),
                       reads=[("p3ps2", ss), ("xrb3", ss)], writes=[("x3b", ss)])
                ctx.dma("gpsimd", scr_x[tt * P:(tt + 1) * P, dc * CH:(dc + 1) * CH], x3b[ss][:],
                        reads=[("x3b", ss)], writes=[("scr_x", tt)])
        ctx.barrier()
        ps3.close()

    prep()
    ar["a"] = es.enter_context(nc.sbuf_tensor("arena", [P, KD * TKV], BF16))
    phase1()
    dump_scr_x("d1")
    ffn_phase("p2", final=False)
    dump_scr_x("d2")
    phase3()
    dump_scr_x("d3")
    ffn_phase("p4", final=True)

    def final_waits(eng, q):
        def f(g):
            for slot in range(KPOOL):
                v = ctx._slot_total(q, slot)
                if v > 0:
                    g.wait_ge(ctx.dsem[q][slot], v)
        return f

    ctx.raw("gpsimd", final_waits("gpsimd", "gpsimd"))
    ctx.raw("sync", final_waits("sync", "sync"))
    ctx.raw("scalar", final_waits("scalar", "scalar"))

    with nc.Block() as block:
        ctx.finalize(block)
    es.close()
    return nc


def make_consts(cfg):
    CHc = cfg["CH"]; SLOTS = CHc // P
    identb = np.eye(P, dtype=ml_dtypes.bfloat16)
    identf = np.eye(P, dtype=np.float32)
    mtri = np.full((SLOTS * P, CHc), -1e30, dtype=ml_dtypes.bfloat16)
    for s in range(SLOTS):
        for jl in range(P):
            il0 = jl + s * P
            if il0 < CHc:
                mtri[s * P + jl, il0:] = 0.0
    return identb, identf, mtri


def make_in_maps(cfg, inputs, nonce=None):
    if nonce is None:
        nonce = int(np.random.randint(1, 2 ** 30))
    D = cfg["D"]; TOWN = cfg["TOWN"]; NC = cfg["NCORES"]
    HA = D // 64; HB = D // P
    identb, identf, mtri = make_consts(cfg)
    x = np.asarray(inputs["L0_attn_a_x"], dtype=np.float32)
    f32 = lambda a: np.ascontiguousarray(np.asarray(a, dtype=np.float32))
    i8 = lambda a: np.ascontiguousarray(np.asarray(a, dtype=np.int8))
    shared = {
        "wqa": f32(inputs["L0_attn_a_Wq"]), "wka": f32(inputs["L0_attn_a_Wk"]),
        "wva": f32(inputs["L0_attn_a_Wv"]), "woa": f32(inputs["L0_attn_a_Wo"]),
        "wf": f32(inputs["L0_attn_a_Wf"]), "bfv": f32(inputs["L0_attn_a_bf"]).reshape(HA, 1),
        "wqb": f32(inputs["L0_attn_b_Wq"]), "wvb": f32(inputs["L0_attn_b_Wv"]),
        "wob": f32(inputs["L0_attn_b_Wo"]), "wgb": f32(inputs["L0_attn_b_Wg"]),
        "lng": f32(inputs["L0_attn_b_ln_gamma"]).reshape(1, D),
        "lnb": f32(inputs["L0_attn_b_ln_beta"]).reshape(1, D),
        "betab": f32(inputs["L0_attn_b_beta"]).reshape(1, HB),
        "wg8": i8(inputs["L0_ffn_w_gate"]), "wu8": i8(inputs["L0_ffn_w_up"]),
        "wd8": i8(inputs["L0_ffn_w_down"]),
        "sg": f32(inputs["L0_ffn_s_gate"]).reshape(1, -1),
        "su": f32(inputs["L0_ffn_s_up"]).reshape(1, -1),
        "sd": f32(inputs["L0_ffn_s_down"]).reshape(1, -1),
        "identb": identb, "identf": identf, "mtri": mtri,
    }
    in_maps = []
    for c in range(NC):
        b, hh = c // 2, c % 2
        xb = np.ascontiguousarray(x[b])
        xo = np.ascontiguousarray(xb[hh * TOWN:(hh + 1) * TOWN])
        xk = np.ascontiguousarray(np.concatenate([xb[0:TOWN], xo], axis=0))
        role = np.full((P, 1), float(hh), dtype=np.float32)
        TKV = xk.shape[0]
        NTB = TKV // P
        HAc = D // 64
        TGAP = TKV - TOWN
        mp = np.zeros((P, NTB * HAc), dtype=np.float32)
        if hh == 0:
            for jb in range(TGAP // P):
                mp[:, jb * HAc:(jb + 1) * HAc] = -1e30
        pair = b
        trash = (NC // 2) * P
        wi = (np.arange(P, dtype=np.int32) + (pair * P if hh == 0 else trash)).reshape(P, 1)
        ri = (np.arange(P, dtype=np.int32) + pair * P).reshape(P, 1)
        non = np.full((P, 16), nonce, dtype=np.int32)
        m = dict(shared)
        m.update({"xown": xo, "xkv": xk, "rolev": role, "mpref": mp,
                  "widx": wi, "ridx": ri, "noncev": non})
        in_maps.append(m)
    return in_maps


def kernel(**inputs):
    cfg = full_cfg()
    nc = build_program(cfg)
    in_maps = make_in_maps(cfg, inputs)
    res = run_bass_kernel_spmd(nc, in_maps, core_ids=list(range(cfg["NCORES"])))
    x = np.asarray(inputs["L0_attn_a_x"], dtype=np.float32)
    outp = np.empty_like(x)
    TOWN = cfg["TOWN"]
    for c in range(cfg["NCORES"]):
        b, hh = c // 2, c % 2
        outp[b, hh * TOWN:(hh + 1) * TOWN] = res.results[c]["out"]
    return outp


# version 04 (final)
# speedup vs baseline: 1.0206x; 1.0206x over previous
import math
from contextlib import ExitStack

import numpy as np
import ml_dtypes

import concourse.bass as bass
import concourse.mybir as mybir
from concourse.bass_utils import run_bass_kernel_spmd

F32 = mybir.dt.float32
BF16 = mybir.dt.bfloat16
I8 = mybir.dt.int8
AF = mybir.ActivationFunctionType
ALU = mybir.AluOpType
AX = mybir.AxisListType
P = 128
EPS = 1e-5
LN2 = math.log(2.0)


def full_cfg():
    return dict(D=2048, TOWN=2048, TKV=4096, CH=512, F=8192, NCORES=8)


def small_cfg():
    return dict(D=256, TOWN=256, TKV=512, CH=128, F=512, NCORES=2)


ENGINES = ("tensor", "scalar", "vector", "gpsimd", "sync")


KPOOL = 6
DQUEUES = ("sync", "gpsimd", "scalar")


class Ctx:
    def __init__(self, nc, es):
        self.nc = nc
        self.esem = {e: es.enter_context(nc.semaphore("es_" + e)) for e in ENGINES}
        self.dsem = {q: [es.enter_context(nc.semaphore(f"ds_{q}{i}")) for i in range(KPOOL)]
                     for q in DQUEUES}
        self.ecount = {e: 0 for e in ENGINES}
        self.dcount = {q: 0 for q in DQUEUES}
        self.streams = {e: [] for e in ENGINES}
        self.observed = {e: {} for e in ENGINES}
        self.regions = {}

    def _slot_total(self, q, slot):
        n = self.dcount[q]
        if n <= slot:
            return 0
        return 16 * ((n - 1 - slot) // KPOOL + 1)

    def _edges(self, eng, reads, writes):
        need = {}

        def add(tok):
            if tok is None:
                return
            kind, src, cnt = tok
            if kind == "e" and src == "tensor" and eng == "tensor":
                return
            if kind == "d":
                q, n = src, cnt
                slot = n % KPOOL
                k = ("d", q, slot)
                v = 16 * (n // KPOOL + 1)
                if need.get(k, 0) < v:
                    need[k] = v
                return
            k = (kind, src)
            if need.get(k, 0) < cnt:
                need[k] = cnt

        for key in reads:
            r = self.regions.get(key)
            if r is not None:
                add(r["w"])
        for key in writes:
            r = self.regions.get(key)
            if r is not None:
                add(r["w"])
                for t in r["r"]:
                    add(t)
        waits = []
        obs = self.observed[eng]
        for k, cnt in sorted(need.items(), key=str):
            if obs.get(k, 0) >= cnt:
                continue
            if k[0] == "e":
                sem = self.esem[k[1]]
            else:
                sem = self.dsem[k[1]][k[2]]
            waits.append((sem, cnt))
            obs[k] = cnt
        return waits

    def _record(self, eng, waits, emit):
        def run(engobj, waits=waits, emit=emit):
            for sem, cnt in waits:
                engobj.wait_ge(sem, cnt)
            emit(engobj)

        self.streams[eng].append(run)

    def op(self, eng, fn, reads=(), writes=()):
        waits = self._edges(eng, reads, writes)
        self.ecount[eng] += 1
        tok = ("e", eng, self.ecount[eng])
        sem = self.esem[eng]

        def emit(engobj, fn=fn, sem=sem):
            fn(engobj).then_inc(sem, 1)

        self._record(eng, waits, emit)
        for key in writes:
            self.regions[key] = {"w": tok, "r": []}
        for key in reads:
            self.regions.setdefault(key, {"w": None, "r": []})["r"].append(tok)
        return tok

    def dma(self, q, out, in_, reads=(), writes=()):
        waits = self._edges(q, reads, writes)
        n = self.dcount[q]
        slot = n % KPOOL
        if n >= KPOOL:
            k = ("d", q, slot)
            v = 16 * ((n - KPOOL) // KPOOL + 1)
            obs = self.observed[q]
            if obs.get(k, 0) < v:
                waits = waits + [(self.dsem[q][slot], v)]
                obs[k] = v
        self.dcount[q] = n + 1
        tok = ("d", q, n)
        sem = self.dsem[q][slot]

        def emit(engobj, out=out, in_=in_, sem=sem):
            engobj.dma_start(out=out, in_=in_).then_inc(sem, 16)

        self._record(q, waits, emit)
        for key in writes:
            self.regions[key] = {"w": tok, "r": []}
        for key in reads:
            self.regions.setdefault(key, {"w": None, "r": []})["r"].append(tok)
        return tok

    def raw(self, eng, fn):
        self.streams[eng].append(fn)

    def barrier(self):
        toks = [(("e", e), self.ecount[e], self.esem[e]) for e in ENGINES if self.ecount[e] > 0]
        for q in DQUEUES:
            for slot in range(KPOOL):
                v = self._slot_total(q, slot)
                if v > 0:
                    toks.append((("d", q, slot), v, self.dsem[q][slot]))
        for eng in ENGINES:
            obs = self.observed[eng]
            waits = []
            for k, cnt, sem in toks:
                if k == ("e", eng):
                    continue
                if obs.get(k, 0) >= cnt:
                    continue
                waits.append((sem, cnt))
                obs[k] = cnt
            if waits:
                self._record(eng, waits, lambda e: None)

    def finalize(self, block):
        for eng in ENGINES:
            ops = self.streams[eng]
            getattr(block, eng)(lambda engobj, ops=ops: [f(engobj) for f in ops])


def build_program(cfg, dumps=()):
    D = cfg["D"]; TOWN = cfg["TOWN"]; TKV = cfg["TKV"]; CH = cfg["CH"]; F = cfg["F"]
    KD = D // P
    NTT = TOWN // P
    NTB = TKV // P
    NCO = TOWN // CH
    NCK = TKV // CH
    TGAP = TKV - TOWN
    SLOTS = CH // P
    HA = D // 64
    HB = D // P
    NF = F // P
    DCH = D // CH
    HALF = KD * TOWN

    def jbmax(ic):
        return (TGAP + (ic + 1) * CH) // P - 1

    def bandst(ic):
        return (TGAP + ic * CH) // P

    nc = bass.Bass("TRN2", target_bir_lowering=False, debug=False)

    def din(name, shape, dt=F32):
        return nc.dram_tensor(name, list(shape), dt, kind="ExternalInput").ap()

    xown = din("xown", (TOWN, D))
    xkv = din("xkv", (TKV, D))
    wsrc = {nm: din(nm, (D, D)) for nm in ("wqa", "wka", "wva", "woa", "wqb", "wvb", "wob", "wgb")}
    wf = din("wf", (HA, D)); bfv = din("bfv", (HA, 1))
    lng = din("lng", (1, D)); lnb = din("lnb", (1, D)); betab = din("betab", (1, HB))
    wg8 = din("wg8", (D, F), I8); wu8 = din("wu8", (D, F), I8); wd8 = din("wd8", (F, D), I8)
    sg = din("sg", (1, F)); su = din("su", (1, F)); sd = din("sd", (1, D))
    identb = din("identb", (P, P), BF16)
    identf = din("identf", (P, P), F32)
    mtri = din("mtri", (SLOTS * P, CH), BF16)
    rolev = din("rolev", (P, 1))
    mpref = din("mpref", (P, (TKV // P) * (D // 64)))
    widx = din("widx", (P, 1), mybir.dt.int32)
    ridx = din("ridx", (P, 1), mybir.dt.int32)
    noncev = din("noncev", (P, 16), mybir.dt.int32)

    out = nc.dram_tensor("out", [TOWN, D], F32, kind="ExternalOutput").ap()
    dump_aps = {nm: nc.dram_tensor(nm, [TOWN, D], F32, kind="ExternalOutput").ap() for nm in dumps}

    scr_x = nc.dram_tensor("scr_x", [TOWN, D], F32).ap()
    scr_q = nc.dram_tensor("scr_q", [HA * 67, TOWN], BF16).ap()
    scr_k = nc.dram_tensor("scr_k", [HA * 64, TKV], BF16).ap()
    scr_v = nc.dram_tensor("scr_v", [TKV, HA * 64], BF16).ap()
    scr_g = nc.dram_tensor("scr_g", [TOWN, D], BF16).ap()
    scr_qb = nc.dram_tensor("scr_qb", [D, TOWN], BF16).ap()
    scr_vb = nc.dram_tensor("scr_vb", [D, TOWN], BF16).ap()
    wT = {nm: nc.dram_tensor("scrT_" + nm, [KD * KD * P, P], BF16).ap() for nm in wsrc}
    scr_wg = nc.dram_tensor("scr_wg", [NF * KD * P, P], BF16).ap()
    scr_wu = nc.dram_tensor("scr_wu", [NF * KD * P, P], BF16).ap()
    scr_wd = nc.dram_tensor("scr_wd", [F, D], BF16).ap()
    NPAIR = max(1, cfg["NCORES"] // 2)
    shc = nc.dram_tensor("shc", [(NPAIR + 1) * P, HB], F32, addr_space="Shared").ap()
    shf = nc.dram_tensor("shf", [(NPAIR + 1) * P, 16], mybir.dt.int32, addr_space="Shared").ap()

    es = ExitStack()
    ctx = Ctx(nc, es)
    ccsem = es.enter_context(nc.semaphore("ccsem"))

    identb_sb = es.enter_context(nc.sbuf_tensor("identb_sb", [P, P], BF16))
    identf_sb = es.enter_context(nc.sbuf_tensor("identf_sb", [P, P], F32))
    onesf = es.enter_context(nc.sbuf_tensor("onesf", [1, P], F32))
    mtri_sb = es.enter_context(nc.sbuf_tensor("mtri_sb", [P, SLOTS * CH], BF16))
    rolev_sb = es.enter_context(nc.sbuf_tensor("rolev_sb", [P, 1], F32))
    rolenv_sb = es.enter_context(nc.sbuf_tensor("rolenv_sb", [P, 1], F32))

    ctx.dma("sync", identb_sb[:], identb[:], writes=[("identb",)])
    ctx.dma("sync", identf_sb[:], identf[:], writes=[("identf",)])
    ctx.dma("sync", rolev_sb[:], rolev[:], writes=[("rolev",)])
    for s in range(SLOTS):
        ctx.dma("sync", mtri_sb[:, s * CH:(s + 1) * CH], mtri[s * P:(s + 1) * P, :], writes=[("mtri", s)])
    ctx.op("vector", lambda v: v.memset(onesf[:], 1.0), writes=[("onesf",)])
    ctx.op("vector", lambda v: v.tensor_scalar(out=rolenv_sb[:], in0=rolev_sb[:], scalar1=-1.0, scalar2=1.0,
                                               op0=ALU.mult, op1=ALU.add),
           reads=[("rolev",)], writes=[("rolenv",)])

    ar = {}

    def Akv(k, c0, c1):
        return ar["a"][:, k * TKV + c0: k * TKV + c1]

    def Ao(k, c0, c1):
        return ar["a"][:, k * TOWN + c0: k * TOWN + c1]

    def Bv(c0, c1, r0=0, r1=P):
        return ar["a"][r0:r1, HALF + c0: HALF + c1]

    def Wr1(k, c0, c1):
        return ar["a"][:, k * D + c0: k * D + c1]

    def Wr2(k, c0, c1):
        return ar["a"][:, HALF + k * D + c0: HALF + k * D + c1]

    def prep():
        ps = ExitStack()
        s8 = [ps.enter_context(nc.sbuf_tensor(f"p_s8_{i}", [P, F], I8)) for i in range(2)]
        b16 = [ps.enter_context(nc.sbuf_tensor(f"p_b16_{i}", [P, F], BF16)) for i in range(2)]
        scbc = ps.enter_context(nc.sbuf_tensor("p_scbc", [P, F], F32))
        scrow = ps.enter_context(nc.sbuf_tensor("p_scrow", [1, F], F32))
        psb = ps.enter_context(nc.psum_tensor("p_psb", [P, CH], F32))
        wrow = [ps.enter_context(nc.sbuf_tensor(f"p_wrow_{i}", [P, D], F32)) for i in range(2)]
        wrowb = [ps.enter_context(nc.sbuf_tensor(f"p_wrowb_{i}", [P, D], BF16)) for i in range(2)]
        pstr = [ps.enter_context(nc.psum_tensor(f"p_pstr_{i}", [P, P], BF16)) for i in range(4)]
        tbuf = [ps.enter_context(nc.sbuf_tensor(f"p_tbuf_{i}", [P, P], BF16)) for i in range(4)]

        for nm, src, scale, dst, rows, cols, tiled in (
            ("wg", wg8, sg, scr_wg, KD, F, True),
            ("wu", wu8, su, scr_wu, KD, F, True),
            ("wd", wd8, sd, scr_wd, NF, D, False),
        ):
            ctx.dma("sync", scrow[:, 0:cols], scale[:, 0:cols], writes=[("scrow",)])
            for c in range(0, cols, CH):
                ctx.op("tensor", lambda t, c=c: t.matmul(out=psb[:], lhsT=onesf[:], rhs=scrow[0:1, c:c + CH],
                                                         start=True, stop=True),
                       reads=[("onesf",), ("scrow",)], writes=[("p_psb",)])
                ctx.op("scalar", lambda s, c=c: s.copy(out=scbc[:, c:c + CH], in_=psb[:]),
                       reads=[("p_psb",)], writes=[("scbc", c // CH)])
            sckeys = [("scbc", c) for c in range(cols // CH)]
            for kt in range(rows):
                sl = kt % 2
                ctx.dma("sync", s8[sl][:, 0:cols], src[kt * P:(kt + 1) * P, 0:cols], writes=[("p_s8", sl)])
                ctx.op("vector", lambda v, sl=sl, cols=cols: v.tensor_copy(out=b16[sl][:, 0:cols],
                                                                           in_=s8[sl][:, 0:cols]),
                       reads=[("p_s8", sl)], writes=[("p_b16", sl)])
                ctx.op("vector", lambda v, sl=sl, cols=cols: v.tensor_tensor(out=b16[sl][:, 0:cols],
                                                                             in0=b16[sl][:, 0:cols],
                                                                             in1=scbc[:, 0:cols], op=ALU.mult),
                       reads=[("p_b16", sl)] + sckeys, writes=[("p_b16", sl)])
                if tiled:
                    for ft in range(cols // P):
                        ctx.dma("gpsimd", dst[(ft * rows + kt) * P:(ft * rows + kt + 1) * P, :],
                                b16[sl][:, ft * P:(ft + 1) * P],
                                reads=[("p_b16", sl)], writes=[("scrw_" + nm, kt, ft)])
                else:
                    ctx.dma("gpsimd", dst[kt * P:(kt + 1) * P, :], b16[sl][:, 0:cols],
                            reads=[("p_b16", sl)], writes=[("scrw_" + nm, kt)])

        for nm in wsrc:
            src = wsrc[nm]; dst = wT[nm]
            for dot in range(KD):
                sl = dot % 2
                ctx.dma("sync", wrow[sl][:], src[dot * P:(dot + 1) * P, :], writes=[("p_wrow", sl)])
                ctx.op("scalar", lambda s, sl=sl: s.copy(out=wrowb[sl][:], in_=wrow[sl][:]),
                       reads=[("p_wrow", sl)], writes=[("p_wrowb", sl)])
                for dib in range(KD):
                    j = dib % 4
                    ctx.op("tensor", lambda t, sl=sl, j=j, dib=dib:
                           t.transpose(out=pstr[j][:], in_=wrowb[sl][:, dib * P:(dib + 1) * P],
                                       identity=identb_sb[:]),
                           reads=[("p_wrowb", sl), ("identb",)], writes=[("p_pstr", j)])
                    if dib % 2 == 0:
                        ctx.op("scalar", lambda s, j=j: s.copy(out=tbuf[j][:], in_=pstr[j][:]),
                               reads=[("p_pstr", j)], writes=[("p_tbuf", j)])
                    else:
                        ctx.op("vector", lambda v, j=j: v.tensor_copy(out=tbuf[j][:], in_=pstr[j][:]),
                               reads=[("p_pstr", j)], writes=[("p_tbuf", j)])
                    ctx.dma("gpsimd", dst[(dot * KD + dib) * P:(dot * KD + dib + 1) * P, :], tbuf[j][:],
                            reads=[("p_tbuf", j)], writes=[("wT" + nm, dot, dib)])
        ctx.barrier()
        ps.close()

    def ln_rows(xin, st, xb_out, rk, wk, tag, dmp, dmp_key):
        ctx.op("vector", lambda v: v.reduce_sum(out=st[:, 0:1], in_=xin, axis=AX.X),
               reads=rk, writes=[(tag, 0)])
        ctx.op("scalar", lambda s: s.activation(out=dmp, in_=xin, func=AF.Square, accum_out=st[:, 1:2]),
               reads=rk, writes=[dmp_key, (tag, 1)])
        ctx.op("vector", lambda v: v.tensor_scalar(out=st[:, 2:3], in0=st[:, 0:1], scalar1=1.0 / D,
                                                   scalar2=None, op0=ALU.mult),
               reads=[(tag, 0)], writes=[(tag, 2)])
        ctx.op("vector", lambda v: v.tensor_tensor(out=st[:, 3:4], in0=st[:, 2:3], in1=st[:, 2:3], op=ALU.mult),
               reads=[(tag, 2)], writes=[(tag, 3)])
        ctx.op("vector", lambda v: v.tensor_scalar(out=st[:, 4:5], in0=st[:, 1:2], scalar1=1.0 / D,
                                                   scalar2=None, op0=ALU.mult),
               reads=[(tag, 1)], writes=[(tag, 4)])
        ctx.op("vector", lambda v: v.tensor_tensor(out=st[:, 5:6], in0=st[:, 4:5], in1=st[:, 3:4], op=ALU.subtract),
               reads=[(tag, 3), (tag, 4)], writes=[(tag, 5)])
        ctx.op("vector", lambda v: v.tensor_scalar(out=st[:, 5:6], in0=st[:, 5:6], scalar1=float(EPS),
                                                   scalar2=None, op0=ALU.add),
               reads=[(tag, 5)], writes=[(tag, 5)])
        ctx.op("scalar", lambda s: s.activation(out=st[:, 6:7], in_=st[:, 5:6], func=AF.Ln),
               reads=[(tag, 5)], writes=[(tag, 6)])
        ctx.op("scalar", lambda s: s.activation(out=st[:, 6:7], in_=st[:, 6:7], func=AF.Exp, scale=-0.5),
               reads=[(tag, 6)], writes=[(tag, 6)])
        ctx.op("vector", lambda v: v.tensor_tensor(out=st[:, 7:8], in0=st[:, 2:3], in1=st[:, 6:7], op=ALU.mult),
               reads=[(tag, 2), (tag, 6)], writes=[(tag, 7)])
        ctx.op("vector", lambda v: v.tensor_scalar(out=st[:, 7:8], in0=st[:, 7:8], scalar1=-1.0,
                                                   scalar2=None, op0=ALU.mult),
               reads=[(tag, 7)], writes=[(tag, 7)])
        ctx.op("vector", lambda v: v.tensor_scalar(out=xb_out, in0=xin, scalar1=st[:, 6:7],
                                                   scalar2=st[:, 7:8], op0=ALU.mult, op1=ALU.add),
               reads=rk + [(tag, 6), (tag, 7)], writes=[wk])

    def ln_prologue(n_tiles, src_fn, src_keys, dst_fn, dkey, tag):
        ps = ExitStack()
        xbuf = [ps.enter_context(nc.sbuf_tensor(f"{tag}_x{i}", [P, D], F32)) for i in range(2)]
        dmp = ps.enter_context(nc.sbuf_tensor(f"{tag}_dmp", [P, D], F32))
        xb = [ps.enter_context(nc.sbuf_tensor(f"{tag}_xb{i}", [P, D], BF16)) for i in range(2)]
        stt = [ps.enter_context(nc.sbuf_tensor(f"{tag}_st{i}", [P, 8], F32)) for i in range(2)]
        ptr = [ps.enter_context(nc.psum_tensor(f"{tag}_ps{i}", [P, P], BF16)) for i in range(4)]
        for tb in range(n_tiles):
            sl = tb % 2
            ctx.dma("sync", xbuf[sl][:], src_fn(tb), reads=src_keys(tb), writes=[(tag + "x", sl)])
            ln_rows(xbuf[sl][:], stt[sl], xb[sl][:], [(tag + "x", sl)], (tag + "xb", sl), tag + "s" + str(sl),
                    dmp[:], (tag + "dmp",))
            for d in range(KD):
                j = d % 4
                ctx.op("tensor", lambda t, sl=sl, j=j, d=d:
                       t.transpose(out=ptr[j][:], in_=xb[sl][:, d * P:(d + 1) * P], identity=identb_sb[:]),
                       reads=[(tag + "xb", sl), ("identb",)], writes=[(tag + "ps", j)])
                if d % 2 == 0:
                    ctx.op("scalar", lambda s, j=j, d=d, tb=tb: s.copy(out=dst_fn(d, tb), in_=ptr[j][:]),
                           reads=[(tag + "ps", j)], writes=[(dkey, d)])
                else:
                    ctx.op("vector", lambda v, j=j, d=d, tb=tb: v.tensor_copy(out=dst_fn(d, tb), in_=ptr[j][:]),
                           reads=[(tag + "ps", j)], writes=[(dkey, d)])
        ctx.barrier()
        ps.close()

    def dump_scr_x(nm):
        if nm not in dump_aps:
            return
        for tb in range(NTT):
            ctx.dma("gpsimd", dump_aps[nm][tb * P:(tb + 1) * P, :], scr_x[tb * P:(tb + 1) * P, :],
                    reads=[("scr_x", tb)], writes=[("dump", nm, tb)])

    def phase1():
        ln_prologue(NTB, lambda tb: xkv[tb * P:(tb + 1) * P, :], lambda tb: [],
                    lambda d, tb: ar["a"][:, d * TKV + tb * P: d * TKV + (tb + 1) * P], "A", "p1")

        psA = ExitStack()
        ctn = psA.enter_context(nc.sbuf_tensor("c_ctn", [P, NTB * HA], F32))
        ps = ExitStack()
        CW = min(TKV, 1024)
        NW = TKV // CW
        ca = ps.enter_context(nc.sbuf_tensor("c_a", [HA, CW], F32))
        cb = ps.enter_context(nc.sbuf_tensor("c_b", [HA, CW], F32))
        cd = ps.enter_context(nc.sbuf_tensor("c_d", [HA, CW], F32))
        ce = ps.enter_context(nc.sbuf_tensor("c_e", [HA, CW], F32))
        conev = ps.enter_context(nc.sbuf_tensor("c_ones", [HA, CW], F32))
        c8b = ps.enter_context(nc.sbuf_tensor("c_8b", [HA, 3 * CW], BF16))
        carry = ps.enter_context(nc.sbuf_tensor("c_carry", [HA, 1], F32))
        wfrow = ps.enter_context(nc.sbuf_tensor("c_wfrow", [HA, D], F32))
        wfrowb = ps.enter_context(nc.sbuf_tensor("c_wfrowb", [HA, D], BF16))
        wft = ps.enter_context(nc.sbuf_tensor("c_wft", [P, KD * HA], BF16))
        bfsb = ps.enter_context(nc.sbuf_tensor("c_bf", [HA, 1], F32))
        mpref_sb = ps.enter_context(nc.sbuf_tensor("c_mpref", [P, NTB * HA], F32))
        psz = ps.enter_context(nc.psum_tensor("c_psz", [HA, CH], F32))
        pst = ps.enter_context(nc.psum_tensor("c_pst", [P, HA], F32))
        pstb = ps.enter_context(nc.psum_tensor("c_pstb", [P, HA], BF16))

        ctx.dma("sync", wfrow[:], wf[:], writes=[("wfrow",)])
        ctx.dma("sync", bfsb[:], bfv[:], writes=[("bfsb",)])
        ctx.dma("sync", mpref_sb[:], mpref[:], writes=[("mpref",)])
        ctx.op("scalar", lambda s: s.copy(out=wfrowb[:], in_=wfrow[:]), reads=[("wfrow",)], writes=[("wfrowb",)])
        for k in range(KD):
            ctx.op("tensor", lambda t, k=k: t.transpose(out=pstb[:], in_=wfrowb[:, k * P:(k + 1) * P],
                                                        identity=identb_sb[0:HA, 0:HA]),
                   reads=[("wfrowb",), ("identb",)], writes=[("c_pstb",)])
            ctx.op("scalar", lambda s, k=k: s.copy(out=wft[:, k * HA:(k + 1) * HA], in_=pstb[:]),
                   reads=[("c_pstb",)], writes=[("wft", k)])
        ctx.op("vector", lambda v: v.memset(conev[:], 1.0), writes=[("cones",)])
        ctx.op("vector", lambda v: v.memset(carry[:], 0.0), writes=[("ccar",)])
        kA, kB, kD_, kE = ("cwA",), ("cwB",), ("cwD",), ("cwE",)
        for w in range(NW):
            g0 = w * CW
            for ch in range(CW // CH):
                for k in range(KD):
                    ctx.op("tensor", lambda t, g0=g0, ch=ch, k=k:
                           t.matmul(out=psz[:], lhsT=wft[:, k * HA:(k + 1) * HA],
                                    rhs=Akv(k, g0 + ch * CH, g0 + (ch + 1) * CH),
                                    start=(k == 0), stop=(k == KD - 1)),
                           reads=[("wft", k), ("A", k)], writes=[("c_psz",)])
                ctx.op("vector", lambda v, ch=ch: v.tensor_scalar(out=ca[:, ch * CH:(ch + 1) * CH], in0=psz[:],
                                                                  scalar1=bfsb[:], scalar2=None, op0=ALU.add),
                       reads=[("c_psz",), ("bfsb",)], writes=[kA])
            ctx.op("scalar", lambda s: s.activation(out=cb[:], in_=ca[:], func=AF.Abs), reads=[kA], writes=[kB])
            ctx.op("scalar", lambda s: s.activation(out=cb[:], in_=cb[:], func=AF.Exp, scale=-1.0),
                   reads=[kB], writes=[kB])
            ctx.op("vector", lambda v: v.tensor_scalar(out=cb[:], in0=cb[:], scalar1=1.0, scalar2=None,
                                                       op0=ALU.add),
                   reads=[kB], writes=[kB])
            ctx.op("scalar", lambda s: s.activation(out=cb[:], in_=cb[:], func=AF.Ln), reads=[kB], writes=[kB])
            ctx.op("vector", lambda v: v.tensor_scalar_min(out=cd[:], in0=ca[:], scalar1=0.0),
                   reads=[kA], writes=[kD_])
            ctx.op("vector", lambda v: v.tensor_tensor(out=ca[:], in0=cd[:], in1=cb[:], op=ALU.subtract),
                   reads=[kB, kD_], writes=[kA])
            ctx.op("vector", lambda v: v.tensor_tensor_scan(out=ce[:], data0=conev[:], data1=ca[:],
                                                            initial=carry[:], op0=ALU.mult, op1=ALU.add),
                   reads=[("cones",), kA, ("ccar",)], writes=[kE])
            ctx.op("vector", lambda v: v.tensor_copy(out=carry[:], in_=ce[:, CW - 1:CW]),
                   reads=[kE], writes=[("ccar",)])
            ctx.op("vector", lambda v: v.tensor_scalar(out=ca[:], in0=ce[:], scalar1=8.0, scalar2=None,
                                                       op0=ALU.mult),
                   reads=[kE], writes=[kA])
            ctx.op("scalar", lambda s: s.copy(out=c8b[:, 0:CW], in_=ca[:]), reads=[kA], writes=[("c8b", 0)])
            ctx.op("vector", lambda v: v.tensor_copy(out=cb[:], in_=c8b[:, 0:CW]), reads=[("c8b", 0)], writes=[kB])
            ctx.op("vector", lambda v: v.tensor_tensor(out=cd[:], in0=ca[:], in1=cb[:], op=ALU.subtract),
                   reads=[kA, kB], writes=[kD_])
            ctx.op("scalar", lambda s: s.copy(out=c8b[:, CW:2 * CW], in_=cd[:]), reads=[kD_], writes=[("c8b", 1)])
            ctx.op("vector", lambda v: v.tensor_copy(out=cb[:], in_=c8b[:, CW:2 * CW]),
                   reads=[("c8b", 1)], writes=[kB])
            ctx.op("vector", lambda v: v.tensor_tensor(out=cd[:], in0=cd[:], in1=cb[:], op=ALU.subtract),
                   reads=[kD_, kB], writes=[kD_])
            ctx.op("scalar", lambda s: s.copy(out=c8b[:, 2 * CW:3 * CW], in_=cd[:]), reads=[kD_], writes=[("c8b", 2)])
            a0 = max(TGAP, g0); a1 = min(TKV, g0 + CW)
            if a1 > a0:
                for h in range(HA):
                    for si in range(3):
                        ctx.dma("gpsimd", scr_q[h * 67 + 64 + si:h * 67 + 65 + si, a0 - TGAP:a1 - TGAP],
                                c8b[h:h + 1, si * CW + (a0 - g0):si * CW + (a1 - g0)],
                                reads=[("c8b", si)], writes=[("scrq_aug", h, si)])
            ctx.op("vector", lambda v: v.tensor_scalar(out=ce[:], in0=ce[:], scalar1=-1.0, scalar2=None,
                                                       op0=ALU.mult),
                   reads=[kE], writes=[kE])
            for tbl in range(CW // P):
                tb = g0 // P + tbl
                ctx.op("tensor", lambda t, tbl=tbl: t.transpose(out=pst[:], in_=ce[:, tbl * P:(tbl + 1) * P],
                                                                identity=identf_sb[0:HA, 0:HA]),
                       reads=[kE, ("identf",)], writes=[("c_pst",)])
                ctx.op("scalar", lambda s, tb=tb: s.copy(out=ctn[:, tb * HA:(tb + 1) * HA], in_=pst[:]),
                       reads=[("c_pst",)], writes=[("ctn", tb)])
                ctx.op("vector", lambda v, tb=tb: v.tensor_tensor(out=ctn[:, tb * HA:(tb + 1) * HA],
                                                                  in0=ctn[:, tb * HA:(tb + 1) * HA],
                                                                  in1=mpref_sb[:, tb * HA:(tb + 1) * HA],
                                                                  op=ALU.add),
                       reads=[("mpref",), ("ctn", tb)], writes=[("ctn", tb)])
        ctx.barrier()
        ps.close()

        ps2 = ExitStack()
        qwt = [ps2.enter_context(nc.sbuf_tensor(f"pr_qwt{i}", [P, KD * P], BF16)) for i in range(2)]
        pev = [ps2.enter_context(nc.psum_tensor(f"pr_ps{i}", [P, CH], F32)) for i in range(2)]
        qev = [ps2.enter_context(nc.sbuf_tensor(f"pr_ev{i}", [P, CH], BF16)) for i in range(2)]

        for nm, dst, nch, off, rows in (("wqa", scr_q, NCO, TGAP, 67), ("wka", scr_k, NCK, 0, 64)):
            for dot in range(KD):
                wsl = dot % 2
                for k in range(KD):
                    ctx.dma("sync", qwt[wsl][:, k * P:(k + 1) * P],
                            wT[nm][(dot * KD + k) * P:(dot * KD + k + 1) * P, :],
                            reads=[("wT" + nm, dot, k)], writes=[("qwt", wsl, k)])
                for ch in range(nch):
                    ssl = (dot * nch + ch) % 2
                    for k in range(KD):
                        ctx.op("tensor", lambda t, wsl=wsl, ssl=ssl, ch=ch, k=k, off=off:
                               t.matmul(out=pev[ssl][:], lhsT=qwt[wsl][:, k * P:(k + 1) * P],
                                        rhs=Akv(k, off + ch * CH, off + (ch + 1) * CH),
                                        start=(k == 0), stop=(k == KD - 1)),
                               reads=[("qwt", wsl, k), ("A", k)], writes=[("pr_ps", ssl)])
                    ctx.op("scalar", lambda s, ssl=ssl: s.copy(out=qev[ssl][:], in_=pev[ssl][:]),
                           reads=[("pr_ps", ssl)], writes=[("pr_ev", ssl)])
                    for half in range(2):
                        h = dot * 2 + half
                        ctx.dma("gpsimd",
                                dst[h * rows:h * rows + 64, ch * CH:(ch + 1) * CH],
                                qev[ssl][half * 64:half * 64 + 64, :],
                                reads=[("pr_ev", ssl)], writes=[("scrp" + nm, h, ch)])
        vwt = [ps2.enter_context(nc.sbuf_tensor(f"pr_vwt{i}", [P, KD * CH], BF16)) for i in range(2)]
        for dvc in range(DCH):
            wsl = dvc % 2
            for k in range(KD):
                for j in range(SLOTS):
                    dot = dvc * SLOTS + j
                    ctx.dma("sync", vwt[wsl][:, k * CH + j * P:k * CH + (j + 1) * P],
                            wT["wva"][(dot * KD + k) * P:(dot * KD + k + 1) * P, :],
                            reads=[("wTwva", dot, k)], writes=[("vwt", wsl, k, j)])
            for tb in range(NTB):
                ssl = (dvc * NTB + tb) % 2
                for k in range(KD):
                    ctx.op("tensor", lambda t, wsl=wsl, ssl=ssl, tb=tb, k=k:
                           t.matmul(out=pev[ssl][:], lhsT=Akv(k, tb * P, (tb + 1) * P),
                                    rhs=vwt[wsl][:, k * CH:(k + 1) * CH],
                                    start=(k == 0), stop=(k == KD - 1)),
                           reads=[("vwt", wsl, k, j) for j in range(SLOTS)] + [("A", k)],
                           writes=[("pr_ps", ssl)])
                ctx.op("scalar", lambda s, ssl=ssl: s.copy(out=qev[ssl][:], in_=pev[ssl][:]),
                       reads=[("pr_ps", ssl)], writes=[("pr_ev", ssl)])
                ctx.dma("gpsimd", scr_v[tb * P:(tb + 1) * P, dvc * CH:(dvc + 1) * CH], qev[ssl][:],
                        reads=[("pr_ev", ssl)], writes=[("scr_v", tb, dvc)])
        ctx.barrier()
        ps2.close()

        ps3 = ExitStack()
        qh = [ps3.enter_context(nc.sbuf_tensor(f"at_qh{i}", [67, TOWN], BF16)) for i in range(2)]
        kh = [ps3.enter_context(nc.sbuf_tensor(f"at_kh{i}", [67, P], BF16)) for i in range(2)]
        vh = [ps3.enter_context(nc.sbuf_tensor(f"at_vh{i}", [P, 65], BF16)) for i in range(2)]
        ptb = [ps3.enter_context(nc.sbuf_tensor(f"at_pt{i}", [P, CH], BF16)) for i in range(2)]
        psS = [ps3.enter_context(nc.psum_tensor(f"at_psS{i}", [P, CH], F32)) for i in range(2)]
        psAV = [ps3.enter_context(nc.psum_tensor(f"at_psAV{i}", [65, CH], F32)) for i in range(NCO)]
        psB = ps3.enter_context(nc.psum_tensor("at_psB", [64, CH], F32))
        avsb = ps3.enter_context(nc.sbuf_tensor("at_avsb", [64, CH], BF16))
        lrow = ps3.enter_context(nc.sbuf_tensor("at_lrow", [1, CH], F32))
        linv = ps3.enter_context(nc.sbuf_tensor("at_linv", [1, CH], F32))
        for i in range(2):
            ctx.op("vector", lambda v, i=i: v.memset(kh[i][64:67, :], 1.0), writes=[("khones", i)])
            ctx.op("vector", lambda v, i=i: v.memset(vh[i][:, 64:65], 1.0), writes=[("vhones", i)])

        for h in range(HA):
            hs = h % 2
            ctx.dma("sync", qh[hs][:], scr_q[h * 67:(h + 1) * 67, :],
                    reads=[("scrpwqa", h, ch) for ch in range(NCO)] +
                          [("scrq_aug", h, si) for si in range(3)],
                    writes=[("qh", hs)])
            for jb in range(NTB):
                bs = (h * NTB + jb) % 2
                ctx.dma("sync", kh[bs][0:64, :], scr_k[h * 64:h * 64 + 64, jb * P:(jb + 1) * P],
                        reads=[("scrpwka", h, jb * P // CH)], writes=[("kh", bs)])
                ctx.dma("sync", vh[bs][:, 0:64], scr_v[jb * P:(jb + 1) * P, h * 64:h * 64 + 64],
                        reads=[("scr_v", jb, (h * 64) // CH)], writes=[("vh", bs)])
                for ic in range(NCO):
                    if jb > jbmax(ic):
                        continue
                    ss = (h * NTB * NCO + jb * NCO + ic) % 2
                    ctx.op("tensor", lambda t, bs=bs, hs=hs, ss=ss, ic=ic:
                           t.matmul(out=psS[ss][:], lhsT=kh[bs][:], rhs=qh[hs][:, ic * CH:(ic + 1) * CH],
                                    start=True, stop=True),
                           reads=[("kh", bs), ("khones", bs), ("qh", hs)], writes=[("psS", ss)])
                    if bandst(ic) <= jb <= jbmax(ic):
                        slot = jb - bandst(ic)
                        ctx.op("vector", lambda v, ss=ss, slot=slot:
                               v.tensor_tensor(out=psS[ss][:], in0=psS[ss][:],
                                               in1=mtri_sb[:, slot * CH:(slot + 1) * CH], op=ALU.add),
                               reads=[("psS", ss), ("mtri", slot)], writes=[("psS", ss)])
                    ctx.op("scalar", lambda s, ss=ss, jb=jb, h=h:
                           s.activation(out=ptb[ss][:], in_=psS[ss][:], func=AF.Exp, scale=0.125,
                                        bias=ctn[:, jb * HA + h:jb * HA + h + 1]),
                           reads=[("psS", ss), ("ctn", jb)], writes=[("ptb", ss)])
                    ctx.op("tensor", lambda t, bs=bs, ic=ic, jb=jb, ss=ss:
                           t.matmul(out=psAV[ic][:], lhsT=vh[bs][:], rhs=ptb[ss][:],
                                    start=(jb == 0), stop=(jb == jbmax(ic))),
                           reads=[("vh", bs), ("vhones", bs), ("ptb", ss)], writes=[("psAV", ic)])
            for ic in range(NCO):
                ctx.op("scalar", lambda s, ic=ic: s.copy(out=lrow[:], in_=psAV[ic][64:65, :]),
                       reads=[("psAV", ic)], writes=[("lrow",)])
                ctx.op("scalar", lambda s, ic=ic: s.copy(out=avsb[:], in_=psAV[ic][0:64, :]),
                       reads=[("psAV", ic)], writes=[("avsb",)])
                ctx.op("vector", lambda v: v.reciprocal(out=linv[:], in_=lrow[:]),
                       reads=[("lrow",)], writes=[("linv",)])
                ctx.op("tensor", lambda t: t.matmul(out=psB[:], lhsT=onesf[:, 0:64], rhs=linv[:],
                                                    start=True, stop=True),
                       reads=[("onesf",), ("linv",)], writes=[("psB",)])
                r0 = (h % 2) * 64
                dcol = (h // 2) * TOWN + ic * CH
                ctx.op("vector", lambda v, r0=r0, dcol=dcol:
                       v.tensor_tensor(out=Bv(dcol, dcol + CH, r0, r0 + 64), in0=avsb[:], in1=psB[:],
                                       op=ALU.mult),
                       reads=[("avsb",), ("psB",)], writes=[("B", h // 2, h % 2)])
        ctx.barrier()
        ps3.close()
        psA.close()

        ps4 = ExitStack()
        for k in range(KD):
            for do in range(KD):
                ctx.dma("sync", Wr1(k, do * P, (do + 1) * P),
                        wT["woa"][(do * KD + k) * P:(do * KD + k + 1) * P, :],
                        reads=[("wTwoa", do, k)], writes=[("W1", k, do)])
        psO = [ps4.enter_context(nc.psum_tensor(f"wo_ps{i}", [P, CH], F32)) for i in range(2)]
        xob = [ps4.enter_context(nc.sbuf_tensor(f"wo_xo{i}", [P, CH], F32)) for i in range(2)]
        x1b = [ps4.enter_context(nc.sbuf_tensor(f"wo_x1{i}", [P, CH], F32)) for i in range(2)]
        for tt in range(NTT):
            for dc in range(DCH):
                ss = (tt * DCH + dc) % 2
                for k in range(KD):
                    ctx.op("tensor", lambda t, ss=ss, k=k, tt=tt, dc=dc:
                           t.matmul(out=psO[ss][:], lhsT=Bv(k * TOWN + tt * P, k * TOWN + (tt + 1) * P),
                                    rhs=Wr1(k, dc * CH, (dc + 1) * CH),
                                    start=(k == 0), stop=(k == KD - 1)),
                           reads=[("B", k, 0), ("B", k, 1)] +
                                 [("W1", k, dc * SLOTS + j) for j in range(SLOTS)],
                           writes=[("psO", ss)])
                ctx.dma("sync", xob[ss][:], xown[tt * P:(tt + 1) * P, dc * CH:(dc + 1) * CH],
                        writes=[("xob", ss)])
                ctx.op("vector", lambda v, ss=ss: v.tensor_tensor(out=x1b[ss][:], in0=psO[ss][:], in1=xob[ss][:],
                                                                  op=ALU.add),
                       reads=[("psO", ss), ("xob", ss)], writes=[("x1b", ss)])
                ctx.dma("gpsimd", scr_x[tt * P:(tt + 1) * P, dc * CH:(dc + 1) * CH], x1b[ss][:],
                        reads=[("x1b", ss)], writes=[("scr_x", tt)])
        ctx.barrier()
        ps4.close()

    def ffn_phase(tagp, final):
        ln_prologue(NTT, lambda tb: scr_x[tb * P:(tb + 1) * P, :], lambda tb: [("scr_x", tb)],
                    lambda d, tb: Ao(d, tb * P, (tb + 1) * P), "A", tagp)
        ps = ExitStack()
        wgt = [ps.enter_context(nc.sbuf_tensor(f"{tagp}_wgt{i}", [P, KD * P], BF16)) for i in range(2)]
        wut = [ps.enter_context(nc.sbuf_tensor(f"{tagp}_wut{i}", [P, KD * P], BF16)) for i in range(2)]
        wdt = [ps.enter_context(nc.sbuf_tensor(f"{tagp}_wdt{i}", [P, CH], BF16)) for i in range(2)]
        psG = ps.enter_context(nc.psum_tensor(tagp + "_psG", [P, CH], F32))
        psU = ps.enter_context(nc.psum_tensor(tagp + "_psU", [P, CH], F32))
        gb = ps.enter_context(nc.sbuf_tensor(tagp + "_gb", [P, CH], BF16))
        ub = ps.enter_context(nc.sbuf_tensor(tagp + "_ub", [P, CH], BF16))
        psD = [ps.enter_context(nc.psum_tensor(f"{tagp}_psD{i}", [P, CH], F32)) for i in range(SLOTS)]
        xrb = [ps.enter_context(nc.sbuf_tensor(f"{tagp}_xr{i}", [P, CH], F32)) for i in range(2)]
        xnb = [ps.enter_context(nc.sbuf_tensor(f"{tagp}_xn{i}", [P, CH], F32)) for i in range(2)]
        dst = out if final else scr_x

        for ts in range(NCO):
            for ft in range(NF):
                wsl = ft % 2
                for k in range(KD):
                    ctx.dma("sync", wgt[wsl][:, k * P:(k + 1) * P],
                            scr_wg[(ft * KD + k) * P:(ft * KD + k + 1) * P, :],
                            reads=[("scrw_wg", k, ft)], writes=[("wgt", wsl, k)])
                    ctx.dma("gpsimd", wut[wsl][:, k * P:(k + 1) * P],
                            scr_wu[(ft * KD + k) * P:(ft * KD + k + 1) * P, :],
                            reads=[("scrw_wu", k, ft)], writes=[("wut", wsl, k)])
                for k in range(KD):
                    ctx.op("tensor", lambda t, wsl=wsl, k=k, ts=ts:
                           t.matmul(out=psG[:], lhsT=wgt[wsl][:, k * P:(k + 1) * P],
                                    rhs=Ao(k, ts * CH, (ts + 1) * CH), start=(k == 0), stop=(k == KD - 1)),
                           reads=[("wgt", wsl, k), ("A", k)], writes=[("psG",)])
                for k in range(KD):
                    ctx.op("tensor", lambda t, wsl=wsl, k=k, ts=ts:
                           t.matmul(out=psU[:], lhsT=wut[wsl][:, k * P:(k + 1) * P],
                                    rhs=Ao(k, ts * CH, (ts + 1) * CH), start=(k == 0), stop=(k == KD - 1)),
                           reads=[("wut", wsl, k), ("A", k)], writes=[("psU",)])
                ctx.op("scalar", lambda s: s.activation(out=gb[:], in_=psG[:], func=AF.Silu),
                       reads=[("psG",)], writes=[("gb",)])
                ctx.op("scalar", lambda s: s.copy(out=ub[:], in_=psU[:]),
                       reads=[("psU",)], writes=[("ub",)])
                ctx.op("vector", lambda v, ft=ft: v.tensor_tensor(out=Bv(ft * CH, (ft + 1) * CH),
                                                                  in0=gb[:], in1=ub[:], op=ALU.mult),
                       reads=[("gb",), ("ub",)], writes=[("B", ft)])
            for hc in range(DCH):
                for ft in range(NF):
                    wsl = ft % 2
                    ctx.dma("sync" if ft % 2 == 0 else "gpsimd", wdt[wsl][:],
                            scr_wd[ft * P:(ft + 1) * P, hc * CH:(hc + 1) * CH],
                            reads=[("scrw_wd", ft)], writes=[("wdt", wsl)])
                    for tl in range(SLOTS):
                        ctx.op("tensor", lambda t, wsl=wsl, ft=ft, tl=tl:
                               t.matmul(out=psD[tl][:], lhsT=Bv(ft * CH + tl * P, ft * CH + (tl + 1) * P),
                                        rhs=wdt[wsl][:], start=(ft == 0), stop=(ft == NF - 1)),
                               reads=[("wdt", wsl), ("B", ft)], writes=[("psD", tl)])
                for tl in range(SLOTS):
                    rsl = (hc * SLOTS + tl) % 2
                    row0 = ts * CH + tl * P
                    ctx.dma("sync", xrb[rsl][:], scr_x[row0:row0 + P, hc * CH:(hc + 1) * CH],
                            reads=[("scr_x", row0 // P)], writes=[("xrb", rsl)])
                    ctx.op("vector", lambda v, rsl=rsl, tl=tl:
                           v.tensor_tensor(out=xnb[rsl][:], in0=psD[tl][:], in1=xrb[rsl][:], op=ALU.add),
                           reads=[("psD", tl), ("xrb", rsl)], writes=[("xnb", rsl)])
                    ctx.dma("gpsimd", dst[row0:row0 + P, hc * CH:(hc + 1) * CH], xnb[rsl][:],
                            reads=[("xnb", rsl)],
                            writes=[("scr_x", row0 // P) if not final else ("outw", row0 // P, hc)])
        ctx.barrier()
        ps.close()

    def phase3():
        ln_prologue(NTT, lambda tb: scr_x[tb * P:(tb + 1) * P, :], lambda tb: [("scr_x", tb)],
                    lambda d, tb: Ao(d, tb * P, (tb + 1) * P), "A", "p3")
        ps = ExitStack()
        psP = [ps.enter_context(nc.psum_tensor(f"p3_pp{i}", [P, CH], F32)) for i in range(2)]
        evb = [ps.enter_context(nc.sbuf_tensor(f"p3_ev{i}", [P, CH], BF16)) for i in range(2)]
        for k in range(KD):
            for do in range(KD):
                ctx.dma("sync", Wr2(k, do * P, (do + 1) * P),
                        wT["wgb"][(do * KD + k) * P:(do * KD + k + 1) * P, :],
                        reads=[("wTwgb", do, k)], writes=[("W3a", k, do)])
        for tt in range(NTT):
            for dc in range(DCH):
                ss = (tt * DCH + dc) % 2
                for k in range(KD):
                    ctx.op("tensor", lambda t, ss=ss, k=k, tt=tt, dc=dc:
                           t.matmul(out=psP[ss][:], lhsT=Ao(k, tt * P, (tt + 1) * P),
                                    rhs=Wr2(k, dc * CH, (dc + 1) * CH), start=(k == 0), stop=(k == KD - 1)),
                           reads=[("A", k)] + [("W3a", k, dc * SLOTS + j) for j in range(SLOTS)],
                           writes=[("p3ps", ss)])
                ctx.op("scalar", lambda s, ss=ss: s.activation(out=evb[ss][:], in_=psP[ss][:], func=AF.Silu),
                       reads=[("p3ps", ss)], writes=[("p3ev", ss)])
                ctx.dma("gpsimd", scr_g[tt * P:(tt + 1) * P, dc * CH:(dc + 1) * CH], evb[ss][:],
                        reads=[("p3ev", ss)], writes=[("scr_g", tt, dc)])
        qwt = [ps.enter_context(nc.sbuf_tensor(f"p3_qwt{i}", [P, KD * P], BF16)) for i in range(2)]
        for nm, dstt in (("wqb", scr_qb), ("wvb", scr_vb)):
            for dot in range(KD):
                wsl = dot % 2
                for k in range(KD):
                    ctx.dma("sync", qwt[wsl][:, k * P:(k + 1) * P],
                            wT[nm][(dot * KD + k) * P:(dot * KD + k + 1) * P, :],
                            reads=[("wT" + nm, dot, k)], writes=[("p3qwt", wsl, k)])
                for ch in range(NCO):
                    ss = (dot * NCO + ch) % 2
                    for k in range(KD):
                        ctx.op("tensor", lambda t, wsl=wsl, ss=ss, ch=ch, k=k:
                               t.matmul(out=psP[ss][:], lhsT=qwt[wsl][:, k * P:(k + 1) * P],
                                        rhs=Ao(k, ch * CH, (ch + 1) * CH), start=(k == 0), stop=(k == KD - 1)),
                               reads=[("p3qwt", wsl, k), ("A", k)], writes=[("p3ps", ss)])
                    ctx.op("scalar", lambda s, ss=ss: s.copy(out=evb[ss][:], in_=psP[ss][:]),
                           reads=[("p3ps", ss)], writes=[("p3ev", ss)])
                    ctx.dma("gpsimd", dstt[dot * P:(dot + 1) * P, ch * CH:(ch + 1) * CH], evb[ss][:],
                            reads=[("p3ev", ss)], writes=[("scrb" + nm, dot, ch)])
        ctx.barrier()
        ps.close()

        ps2 = ExitStack()
        betasb = ps2.enter_context(nc.sbuf_tensor("p3_beta", [1, HB], F32))
        lamr = ps2.enter_context(nc.sbuf_tensor("p3_lamr", [1, HB], F32))
        lamc = ps2.enter_context(nc.sbuf_tensor("p3_lamc", [P, HB], F32))
        psLC = ps2.enter_context(nc.psum_tensor("p3_psLC", [P, HB], F32))
        onest = ps2.enter_context(nc.sbuf_tensor("p3_onest", [P, TOWN], F32))
        lamt = ps2.enter_context(nc.sbuf_tensor("p3_lamt", [P, TOWN], F32))
        ret1 = ps2.enter_context(nc.sbuf_tensor("p3_ret", [P, TOWN], F32))
        csend = ps2.enter_context(nc.sbuf_tensor("p3_csend", [P, HB], F32))
        crecv = ps2.enter_context(nc.sbuf_tensor("p3_crecv", [P, HB], F32))
        cuse = ps2.enter_context(nc.sbuf_tensor("p3_cuse", [P, HB], F32))
        qbh = [ps2.enter_context(nc.sbuf_tensor(f"p3_qbh{i}", [P, TOWN], BF16)) for i in range(2)]
        oth = [ps2.enter_context(nc.sbuf_tensor(f"p3_oth{i}", [P, TOWN], BF16)) for i in range(2)]
        ptr3 = [ps2.enter_context(nc.psum_tensor(f"p3_ptr{i}", [P, P], BF16)) for i in range(4)]

        ctx.dma("sync", betasb[:], betab[:], writes=[("beta",)])
        ctx.op("scalar", lambda s: s.activation(out=lamr[:], in_=betasb[:], func=AF.Exp, scale=-LN2),
               reads=[("beta",)], writes=[("lamr",)])
        ctx.op("vector", lambda v: v.tensor_scalar(out=lamr[:], in0=lamr[:], scalar1=-1.0, scalar2=1.0,
                                                   op0=ALU.mult, op1=ALU.add),
               reads=[("lamr",)], writes=[("lamr",)])
        ctx.op("vector", lambda v: v.tensor_scalar_max(out=lamr[:], in0=lamr[:], scalar1=1.1754944e-38),
               reads=[("lamr",)], writes=[("lamr",)])
        ctx.op("vector", lambda v: v.tensor_scalar_min(out=lamr[:], in0=lamr[:], scalar1=1.0 - 1e-9),
               reads=[("lamr",)], writes=[("lamr",)])
        ctx.op("tensor", lambda t: t.matmul(out=psLC[:], lhsT=onesf[:], rhs=lamr[:], start=True, stop=True),
               reads=[("onesf",), ("lamr",)], writes=[("psLC",)])
        ctx.op("scalar", lambda s: s.copy(out=lamc[:], in_=psLC[:]), reads=[("psLC",)], writes=[("lamc",)])
        ctx.op("vector", lambda v: v.memset(onest[:], 1.0), writes=[("onest",)])
        for h in range(HB):
            ctx.dma("sync", Bv(h * TOWN, (h + 1) * TOWN), scr_vb[h * P:(h + 1) * P, :],
                    reads=[("scrbwvb", h, ch) for ch in range(NCO)], writes=[("B", h)])
            ctx.op("vector", lambda v, h=h: v.tensor_scalar(out=lamt[:], in0=onest[:],
                                                            scalar1=lamc[:, h:h + 1], scalar2=None, op0=ALU.mult),
                   reads=[("onest",), ("lamc",)], writes=[("lamt",)])
            ctx.op("vector", lambda v, h=h: v.tensor_tensor_scan(out=ret1[:], data0=lamt[:],
                                                                 data1=Bv(h * TOWN, (h + 1) * TOWN),
                                                                 initial=0.0, op0=ALU.mult, op1=ALU.add),
                   reads=[("lamt",), ("B", h)], writes=[("ret1",)])
            ctx.op("vector", lambda v, h=h: v.tensor_tensor(out=csend[:, h:h + 1], in0=ret1[:, TOWN - 1:TOWN],
                                                            in1=rolenv_sb[:], op=ALU.mult),
                   reads=[("ret1",), ("rolenv",)], writes=[("csend", h)])
        swidx = ps2.enter_context(nc.sbuf_tensor("x_widx", [P, 1], mybir.dt.int32))
        sridx = ps2.enter_context(nc.sbuf_tensor("x_ridx", [P, 1], mybir.dt.int32))
        snon = ps2.enter_context(nc.sbuf_tensor("x_non", [P, 16], mybir.dt.int32))
        fpoll = ps2.enter_context(nc.sbuf_tensor("x_fp", [P, 16], mybir.dt.int32))
        crecvr = ps2.enter_context(nc.sbuf_tensor("x_cr", [P, HB], F32))
        ctx.dma("sync", swidx[:], widx[:], writes=[("swidx",)])
        ctx.dma("sync", sridx[:], ridx[:], writes=[("sridx",)])
        ctx.dma("sync", snon[:], noncev[:], writes=[("snon",)])
        ctx.op("gpsimd", lambda g: g.engine_nop(),
               reads=[("csend", h) for h in range(HB)] + [("swidx",), ("sridx",), ("snon",)],
               writes=[])
        IOA = bass.IndirectOffsetOnAxis
        xsem = ctx.esem

        def xchg(g):
            g.indirect_dma_start(out=shc[:], out_offset=IOA(ap=swidx[:], axis=0), in_=csend[:],
                                 in_offset=None, oob_is_err=False).then_inc(ccsem, 16)
            g.wait_ge(ccsem, 16)
            g.indirect_dma_start(out=shf[:], out_offset=IOA(ap=swidx[:], axis=0), in_=snon[:],
                                 in_offset=None, oob_is_err=False).then_inc(ccsem, 16)
            g.wait_ge(ccsem, 32)
            with g.register("xr") as xr, g.register("xr2") as xr2, g.register("xr3") as xr3, \
                    g.register("xthr") as xthr:
                g.reg_load(xr3, snon[0:1, 0:1])
                g.reg_mov(xr, 1)
                g.reg_mov(xthr, 32)
                with g.While(xr):
                    g.reg_add(xthr, xthr, 16)
                    g.indirect_dma_start(out=fpoll[:], out_offset=None, in_=shf[:],
                                         in_offset=IOA(ap=sridx[:], axis=0),
                                         oob_is_err=False).then_inc(ccsem, 16)
                    g.wait_ge(ccsem, xthr)
                    g.reg_load(xr2, fpoll[0:1, 0:1])
                    g.reg_sub(xr2, xr2, xr3)
                    with g.If_eq(xr2, 0):
                        g.reg_mov(xr, 0)
                    with g.Else():
                        g.reg_mov(xr, 1)
                g.reg_add(xthr, xthr, 16)
                g.indirect_dma_start(out=crecvr[:], out_offset=None, in_=shc[:],
                                     in_offset=IOA(ap=sridx[:], axis=0), oob_is_err=False).then_inc(ccsem, 16)
                g.wait_ge(ccsem, xthr)

        ctx.raw("gpsimd", xchg)
        ctx.op("gpsimd", lambda g: g.tensor_copy(out=crecv[:], in_=crecvr[:]),
               reads=[], writes=[("crecv",)])
        ctx.op("vector", lambda v: v.tensor_scalar(out=cuse[:], in0=crecv[:], scalar1=rolev_sb[:],
                                                   scalar2=None, op0=ALU.mult),
               reads=[("crecv",), ("rolev",)], writes=[("cuse",)])

        for h in range(HB):
            hs = h % 2
            ctx.dma("sync", qbh[hs][:], scr_qb[h * P:(h + 1) * P, :],
                    reads=[("scrbwqb", h, ch) for ch in range(NCO)], writes=[("qbh", hs)])
            ctx.op("vector", lambda v, h=h: v.tensor_scalar(out=lamt[:], in0=onest[:],
                                                            scalar1=lamc[:, h:h + 1], scalar2=None, op0=ALU.mult),
                   reads=[("onest",), ("lamc",)], writes=[("lamt",)])
            ctx.op("vector", lambda v, h=h: v.tensor_tensor_scan(out=ret1[:], data0=lamt[:],
                                                                 data1=Bv(h * TOWN, (h + 1) * TOWN),
                                                                 initial=cuse[:, h:h + 1], op0=ALU.mult,
                                                                 op1=ALU.add),
                   reads=[("lamt",), ("B", h), ("cuse",)], writes=[("ret1",)])
            ctx.op("vector", lambda v, hs=hs: v.tensor_tensor(out=oth[hs][:], in0=qbh[hs][:], in1=ret1[:],
                                                              op=ALU.mult),
                   reads=[("qbh", hs), ("ret1",)], writes=[("oth", hs)])
            for tb in range(NTT):
                j = tb % 4
                ctx.op("tensor", lambda t, hs=hs, j=j, tb=tb:
                       t.transpose(out=ptr3[j][:], in_=oth[hs][:, tb * P:(tb + 1) * P], identity=identb_sb[:]),
                       reads=[("oth", hs), ("identb",)], writes=[("p3tr", j)])
                if tb % 2 == 0:
                    ctx.op("scalar", lambda s, j=j, tb=tb, h=h:
                           s.copy(out=ar["a"][:, tb * D + h * P:tb * D + (h + 1) * P], in_=ptr3[j][:]),
                           reads=[("p3tr", j)], writes=[("A3", tb)])
                else:
                    ctx.op("vector", lambda v, j=j, tb=tb, h=h:
                           v.tensor_copy(out=ar["a"][:, tb * D + h * P:tb * D + (h + 1) * P], in_=ptr3[j][:]),
                           reads=[("p3tr", j)], writes=[("A3", tb)])
        ctx.barrier()
        ps2.close()

        ps3 = ExitStack()
        gamb = ps3.enter_context(nc.sbuf_tensor("p3_gamb", [P, D], F32))
        betb = ps3.enter_context(nc.sbuf_tensor("p3_betb", [P, D], F32))
        grow = ps3.enter_context(nc.sbuf_tensor("p3_grow", [1, D], F32))
        st3 = ps3.enter_context(nc.sbuf_tensor("p3_st", [P, 8], F32))
        nnb = ps3.enter_context(nc.sbuf_tensor("p3_nnb", [P, D], F32))
        gt3 = [ps3.enter_context(nc.sbuf_tensor(f"p3_gt{i}", [P, D], BF16)) for i in range(2)]
        zgb = ps3.enter_context(nc.sbuf_tensor("p3_zg", [P, D], BF16))
        ztl = [ps3.enter_context(nc.sbuf_tensor(f"p3_ztl{i}", [P, KD * P], BF16)) for i in range(2)]
        xrb3 = [ps3.enter_context(nc.sbuf_tensor(f"p3_xr{i}", [P, CH], F32)) for i in range(2)]
        x3b = [ps3.enter_context(nc.sbuf_tensor(f"p3_x3{i}", [P, CH], F32)) for i in range(2)]
        ptr4 = [ps3.enter_context(nc.psum_tensor(f"p3_ptr4{i}", [P, P], BF16)) for i in range(4)]
        psP2 = [ps3.enter_context(nc.psum_tensor(f"p3_psP2{i}", [P, CH], F32)) for i in range(2)]

        for k in range(KD):
            for do in range(KD):
                ctx.dma("sync", Wr2(k, do * P, (do + 1) * P),
                        wT["wob"][(do * KD + k) * P:(do * KD + k + 1) * P, :],
                        reads=[("wTwob", do, k)], writes=[("W3f", k, do)])
        for nm, src, dstb in (("lng", lng, gamb), ("lnb", lnb, betb)):
            ctx.dma("sync", grow[:], src[:], writes=[("grow",)])
            for c in range(DCH):
                ctx.op("tensor", lambda t, c=c: t.matmul(out=psP2[0][:], lhsT=onesf[:],
                                                         rhs=grow[0:1, c * CH:(c + 1) * CH],
                                                         start=True, stop=True),
                       reads=[("grow",), ("onesf",)], writes=[("p3ps2", 0)])
                ctx.op("scalar", lambda s, c=c, dstb=dstb: s.copy(out=dstb[:, c * CH:(c + 1) * CH],
                                                                  in_=psP2[0][:]),
                       reads=[("p3ps2", 0)], writes=[(nm, c)])
        gamkeys = [("lng", c) for c in range(DCH)]
        betkeys = [("lnb", c) for c in range(DCH)]
        for tt in range(NTT):
            hs = tt % 2
            otile = ar["a"][:, tt * D:(tt + 1) * D]
            ln_rows(otile, st3, nnb[:], [("A3", tt)], ("nnb",), "p3f", zgb[:], ("zgb",))
            ctx.op("vector", lambda v: v.tensor_tensor(out=nnb[:], in0=nnb[:], in1=gamb[:], op=ALU.mult),
                   reads=[("nnb",)] + gamkeys, writes=[("nnb",)])
            ctx.op("vector", lambda v: v.tensor_tensor(out=nnb[:], in0=nnb[:], in1=betb[:], op=ALU.add),
                   reads=[("nnb",)] + betkeys, writes=[("nnb",)])
            ctx.dma("sync", gt3[hs][:], scr_g[tt * P:(tt + 1) * P, :],
                    reads=[("scr_g", tt, dc) for dc in range(DCH)], writes=[("gt3", hs)])
            ctx.op("vector", lambda v, hs=hs: v.tensor_tensor(out=zgb[:], in0=nnb[:], in1=gt3[hs][:],
                                                              op=ALU.mult),
                   reads=[("nnb",), ("gt3", hs)], writes=[("zgb",)])
            for d in range(KD):
                j = d % 4
                ctx.op("tensor", lambda t, j=j, d=d:
                       t.transpose(out=ptr4[j][:], in_=zgb[:, d * P:(d + 1) * P], identity=identb_sb[:]),
                       reads=[("zgb",), ("identb",)], writes=[("p3tr4", j)])
                if d % 2 == 0:
                    ctx.op("scalar", lambda s, j=j, d=d, hs=hs: s.copy(out=ztl[hs][:, d * P:(d + 1) * P],
                                                                       in_=ptr4[j][:]),
                           reads=[("p3tr4", j)], writes=[("ztl", hs, d)])
                else:
                    ctx.op("vector", lambda v, j=j, d=d, hs=hs: v.tensor_copy(out=ztl[hs][:, d * P:(d + 1) * P],
                                                                              in_=ptr4[j][:]),
                           reads=[("p3tr4", j)], writes=[("ztl", hs, d)])
            for dc in range(DCH):
                ss = (tt * DCH + dc) % 2
                for k in range(KD):
                    ctx.op("tensor", lambda t, ss=ss, k=k, dc=dc, hs=hs:
                           t.matmul(out=psP2[ss][:], lhsT=ztl[hs][:, k * P:(k + 1) * P],
                                    rhs=Wr2(k, dc * CH, (dc + 1) * CH), start=(k == 0), stop=(k == KD - 1)),
                           reads=[("ztl", hs, k)] + [("W3f", k, dc * SLOTS + j) for j in range(SLOTS)],
                           writes=[("p3ps2", ss)])
                ctx.dma("sync", xrb3[ss][:], scr_x[tt * P:(tt + 1) * P, dc * CH:(dc + 1) * CH],
                        reads=[("scr_x", tt)], writes=[("xrb3", ss)])
                ctx.op("vector", lambda v, ss=ss: v.tensor_tensor(out=x3b[ss][:], in0=psP2[ss][:],
                                                                  in1=xrb3[ss][:], op=ALU.add),
                       reads=[("p3ps2", ss), ("xrb3", ss)], writes=[("x3b", ss)])
                ctx.dma("gpsimd", scr_x[tt * P:(tt + 1) * P, dc * CH:(dc + 1) * CH], x3b[ss][:],
                        reads=[("x3b", ss)], writes=[("scr_x", tt)])
        ctx.barrier()
        ps3.close()

    prep()
    ar["a"] = es.enter_context(nc.sbuf_tensor("arena", [P, KD * TKV], BF16))
    phase1()
    dump_scr_x("d1")
    ffn_phase("p2", final=False)
    dump_scr_x("d2")
    phase3()
    dump_scr_x("d3")
    ffn_phase("p4", final=True)

    def final_waits(eng, q):
        def f(g):
            for slot in range(KPOOL):
                v = ctx._slot_total(q, slot)
                if v > 0:
                    g.wait_ge(ctx.dsem[q][slot], v)
        return f

    ctx.raw("gpsimd", final_waits("gpsimd", "gpsimd"))
    ctx.raw("sync", final_waits("sync", "sync"))
    ctx.raw("scalar", final_waits("scalar", "scalar"))

    with nc.Block() as block:
        ctx.finalize(block)
    es.close()
    return nc


def make_consts(cfg):
    CHc = cfg["CH"]; SLOTS = CHc // P
    identb = np.eye(P, dtype=ml_dtypes.bfloat16)
    identf = np.eye(P, dtype=np.float32)
    mtri = np.full((SLOTS * P, CHc), -1e30, dtype=ml_dtypes.bfloat16)
    for s in range(SLOTS):
        for jl in range(P):
            il0 = jl + s * P
            if il0 < CHc:
                mtri[s * P + jl, il0:] = 0.0
    return identb, identf, mtri


def make_in_maps(cfg, inputs, nonce=None):
    if nonce is None:
        nonce = int(np.random.randint(1, 2 ** 30))
    D = cfg["D"]; TOWN = cfg["TOWN"]; NC = cfg["NCORES"]
    HA = D // 64; HB = D // P
    identb, identf, mtri = make_consts(cfg)
    x = np.asarray(inputs["L0_attn_a_x"], dtype=np.float32)
    f32 = lambda a: np.ascontiguousarray(np.asarray(a, dtype=np.float32))
    i8 = lambda a: np.ascontiguousarray(np.asarray(a, dtype=np.int8))
    shared = {
        "wqa": f32(inputs["L0_attn_a_Wq"]), "wka": f32(inputs["L0_attn_a_Wk"]),
        "wva": f32(inputs["L0_attn_a_Wv"]), "woa": f32(inputs["L0_attn_a_Wo"]),
        "wf": f32(inputs["L0_attn_a_Wf"]), "bfv": f32(inputs["L0_attn_a_bf"]).reshape(HA, 1),
        "wqb": f32(inputs["L0_attn_b_Wq"]), "wvb": f32(inputs["L0_attn_b_Wv"]),
        "wob": f32(inputs["L0_attn_b_Wo"]), "wgb": f32(inputs["L0_attn_b_Wg"]),
        "lng": f32(inputs["L0_attn_b_ln_gamma"]).reshape(1, D),
        "lnb": f32(inputs["L0_attn_b_ln_beta"]).reshape(1, D),
        "betab": f32(inputs["L0_attn_b_beta"]).reshape(1, HB),
        "wg8": i8(inputs["L0_ffn_w_gate"]), "wu8": i8(inputs["L0_ffn_w_up"]),
        "wd8": i8(inputs["L0_ffn_w_down"]),
        "sg": f32(inputs["L0_ffn_s_gate"]).reshape(1, -1),
        "su": f32(inputs["L0_ffn_s_up"]).reshape(1, -1),
        "sd": f32(inputs["L0_ffn_s_down"]).reshape(1, -1),
        "identb": identb, "identf": identf, "mtri": mtri,
    }
    in_maps = []
    for c in range(NC):
        b, hh = c // 2, c % 2
        xb = np.ascontiguousarray(x[b])
        xo = np.ascontiguousarray(xb[hh * TOWN:(hh + 1) * TOWN])
        xk = np.ascontiguousarray(np.concatenate([xb[0:TOWN], xo], axis=0))
        role = np.full((P, 1), float(hh), dtype=np.float32)
        TKV = xk.shape[0]
        NTB = TKV // P
        HAc = D // 64
        TGAP = TKV - TOWN
        mp = np.zeros((P, NTB * HAc), dtype=np.float32)
        if hh == 0:
            for jb in range(TGAP // P):
                mp[:, jb * HAc:(jb + 1) * HAc] = -1e30
        pair = b
        trash = (NC // 2) * P
        wi = (np.arange(P, dtype=np.int32) + (pair * P if hh == 0 else trash)).reshape(P, 1)
        ri = (np.arange(P, dtype=np.int32) + pair * P).reshape(P, 1)
        non = np.full((P, 16), nonce, dtype=np.int32)
        m = dict(shared)
        m.update({"xown": xo, "xkv": xk, "rolev": role, "mpref": mp,
                  "widx": wi, "ridx": ri, "noncev": non})
        in_maps.append(m)
    return in_maps


def kernel(**inputs):
    cfg = full_cfg()
    nc = build_program(cfg)
    in_maps = make_in_maps(cfg, inputs)
    res = run_bass_kernel_spmd(nc, in_maps, core_ids=list(range(cfg["NCORES"])))
    x = np.asarray(inputs["L0_attn_a_x"], dtype=np.float32)
    outp = np.empty_like(x)
    TOWN = cfg["TOWN"]
    for c in range(cfg["NCORES"]):
        b, hh = c // 2, c % 2
        outp[b, hh * TOWN:(hh + 1) * TOWN] = res.results[c]["out"]
    return outp
